# version 03
# baseline (speedup 1.0000x reference)
import math
from contextlib import ExitStack

import numpy as np
import ml_dtypes
import concourse.bass as bass
import concourse.mybir as mybir
from concourse.bass_utils import run_bass_kernel_spmd

F32 = mybir.dt.float32
BF16 = mybir.dt.bfloat16
AF = mybir.ActivationFunctionType
ALU = mybir.AluOpType

B, T, C, FF = 4, 8192, 2048, 8192
TH = T // 2
NCT = C // 128
NFT = FF // 128
KAP = 12.0
RSQ = 1.0 / math.sqrt(FF)
EPS = 1e-5
INV_C = 1.0 / C

CH_A = 256
NCH_A = TH // CH_A
CH_B = 512
NCH_B = TH // CH_B
TB = 512
NTB = TH // TB

ENGS = ("s", "g", "t", "v", "a")


class Prog:

    def __init__(self, nc, sem_alloc):
        self.nc = nc
        self.q = {e: [] for e in ENGS}
        self.sem_alloc = sem_alloc
        self.sems = {}
        self.cnt = {}
        for e in ENGS:
            self.sems[e] = sem_alloc("sem_" + e)
            self.cnt[e] = 0
        self.writer = {}
        self.readers = {}
        self.waited = {e: {} for e in ENGS}

    @staticmethod
    def _key(ap):
        t = getattr(ap, "tensor", ap)
        return getattr(t, "name", str(t))

    def buf_sem(self, name):
        sn = "dma_" + name
        if sn not in self.sems:
            self.sems[sn] = self.sem_alloc(sn)
            self.cnt[sn] = 0
        return sn

    def emit(self, eng, f, reads=(), writes=(), dma=False, dma_buf=None):
        waits = {}

        def need(sem_name, val, src_dma, src_eng):
            if self.waited[eng].get(sem_name, -1) >= val:
                return
            waits[sem_name] = max(waits.get(sem_name, 0), val)

        for r in reads:
            w = self.writer.get(self._key(r))
            if w:
                need(*w)
        for wr in writes:
            k = self._key(wr)
            w = self.writer.get(k)
            if w:
                need(*w)
            for rd in self.readers.get(k, ()):
                need(*rd)
        for sn, v in waits.items():
            self.waited[eng][sn] = max(self.waited[eng].get(sn, -1), v)
        if dma:
            assert dma_buf is not None
            sn = self.buf_sem(self._key(dma_buf))
            d = 16
        else:
            sn = eng
            d = 1
        self.cnt[sn] += d
        val = self.cnt[sn]
        my_sem = self.sems[sn]
        wlist = [(self.sems[s2], v) for s2, v in sorted(waits.items())]

        def g(e, f=f, wlist=wlist, my_sem=my_sem, d=d):
            for (h, v) in wlist:
                e.wait_ge(h, v)
            ins = f(e)
            if ins is not None:
                ins.then_inc(my_sem, d)

        self.q[eng].append(g)
        mark = (sn, val, dma, eng)
        for wr in writes:
            k = self._key(wr)
            self.writer[k] = mark
            self.readers[k] = []
        for r in reads:
            self.readers.setdefault(self._key(r), []).append(mark)
        return mark

    def barrier(self):
        snap = dict(self.cnt)
        for eng in ENGS:
            for sn in sorted(snap):
                v = snap[sn]
                if v <= 0 or self.waited[eng].get(sn, -1) >= v:
                    continue
                h = self.sems[sn]
                self.q[eng].append(lambda e, h=h, v=v: e.wait_ge(h, v))
                self.waited[eng][sn] = v

    def flush(self, blk):
        qs = {e: self.q[e] for e in ENGS}
        self.q = {e: [] for e in ENGS}
        if qs["s"]:
            fs = qs["s"]

            @blk.sync
            def _(e, fs=fs):
                for f in fs:
                    f(e)
        if qs["g"]:
            fg = qs["g"]

            @blk.gpsimd
            def _(e, fg=fg):
                for f in fg:
                    f(e)
        if qs["t"]:
            ft = qs["t"]

            @blk.tensor
            def _(e, ft=ft):
                for f in ft:
                    f(e)
        if qs["v"]:
            fv = qs["v"]

            @blk.vector
            def _(e, fv=fv):
                for f in fv:
                    f(e)
        if qs["a"]:
            fa = qs["a"]

            @blk.scalar
            def _(e, fa=fa):
                for f in fa:
                    f(e)


def hadamard(n):
    h = np.array([[1.0]], dtype=np.float32)
    while h.shape[0] < n:
        h = np.block([[h, h], [h, -h]]).astype(np.float32)
    return h


def build_nc():
    nc = bass.Bass("TRN2", target_bir_lowering=False)

    def din(name, shape):
        return nc.dram_tensor(name, shape, F32, kind="ExternalInput").ap()

    x = din("x", [TH, C])
    xf = din("xf", [T, C])
    maskv = din("maskv", [128, 1])
    Wr = din("Wr", [C, C]); Wk = din("Wk", [C, C]); Wv = din("Wv", [C, C])
    Wo = din("Wo", [C, C])
    W1 = din("W1", [FF, C]); W3 = din("W3", [FF, C]); W2 = din("W2", [C, FF])
    mu_r = din("mu_r", [1, C]); mu_k = din("mu_k", [1, C]); mu_v = din("mu_v", [1, C])
    wkv_w = din("wkv_w", [1, C]); wkv_u = din("wkv_u", [1, C])
    H128in = din("H128in", [128, 128])
    I128in = din("I128in", [128, 128])

    yout = nc.dram_tensor("y", [TH, C], F32, kind="ExternalOutput").ap()

    xn_sc = nc.dram_tensor("xn_sc", [NCT, 128, T + 1], BF16, kind="Internal").ap()
    ek_sc = nc.dram_tensor("ek_sc", [NCT, 128, T], F32, kind="Internal").ap()
    y_sc = nc.dram_tensor("y_sc", [NCT, 128, TH], F32, kind="Internal").ap()
    z_sc = nc.dram_tensor("z_sc", [NCT, 128, TH], BF16, kind="Internal").ap()
    hid_sc = nc.dram_tensor("hid_sc", [NFT, 128, TH], BF16, kind="Internal").ap()
    at_sc = nc.dram_tensor("at_sc", [NCT, 128, TH], F32, kind="Internal").ap()

    with ExitStack() as ctx:
        def sb(name, shape, dt):
            return ctx.enter_context(nc.sbuf_tensor(name, shape, dt))

        def psum(name, shape, dt):
            return ctx.enter_context(nc.psum_tensor(name, shape, dt))

        def sem_alloc(name):
            return ctx.enter_context(nc.semaphore(name))
        P = Prog(nc, sem_alloc)

        def _BAR_():
            P.barrier()

        ones_f = sb("ones_f", [1, 1], F32)
        id_f = sb("id_f", [128, 128], F32)
        id_b = sb("id_b", [128, 128], BF16)
        h_b = sb("h_b", [128, 128], BF16)
        rowp = sb("rowp", [1, C], F32)
        rown = sb("rown", [1, C], F32)
        mur_t = sb("mur_t", [128, NCT], F32)
        muk_t = sb("muk_t", [128, NCT], F32)
        muv_t = sb("muv_t", [128, NCT], F32)
        ww_t = sb("ww_t", [128, NCT], F32)
        wu_t = sb("wu_t", [128, NCT], F32)
        dec_t = sb("dec_t", [128, NCT], F32)
        eu_t = sb("eu_t", [128, NCT], F32)
        mask_t = sb("mask_t", [128, 1], F32)
        stN = sb("stN", [128, NCT], F32)
        stD = sb("stD", [128, NCT], F32)
        lnS = sb("lnS", [128, 8], F32)
        lnQ = sb("lnQ", [128, 8], F32)
        lnM = sb("lnM", [128, 8], F32)
        lnR = sb("lnR", [128, 8], F32)
        lnT = sb("lnT", [128, 8], F32)
        epsb = sb("epsb", [128, 1], F32)
        kapb = sb("kapb", [128, 1], F32)

        psA = [psum(f"psA{i}", [128, 512], F32) for i in range(4)]
        psT = [psum(f"psT{i}", [128, 512], F32) for i in range(2)]
        psB = [psum(f"psB{i}", [128, 1024], BF16) for i in range(2)]

        blk = ctx.enter_context(nc.Block())

        def dma(eng, out, in_, reads=(), writes=()):
            t_out = getattr(out, "tensor", out)
            buf = out if "SB" in type(t_out).__name__ else in_
            P.emit(eng, lambda e, o=out, i=in_: e.dma_start(out=o, in_=i),
                   reads=tuple(reads) + (in_,), writes=tuple(writes) + (out,),
                   dma=True, dma_buf=buf)

        def tr_set(n, src_tensor, src_fn, dst_tensor, dst_fn, f32=False):
            per = 4 if f32 else 8
            banks = psT if f32 else psB
            ident = id_f if f32 else id_b
            ev_eng = "v" if f32 else "a"
            for g0 in range(0, n, per):
                cnt = min(per, n - g0)
                bank = banks[(g0 // per) % 2]

                def ftr(e, g0=g0, cnt=cnt, bank=bank):
                    ins = None
                    for j in range(cnt):
                        ins = e.transpose(bank[:, j * 128:(j + 1) * 128],
                                          src_fn(g0 + j), ident[:])
                    return ins
                P.emit("t", ftr, reads=[src_tensor, ident], writes=[bank])
                src3 = bank[:].rearrange("p (g r) -> p g r", r=128)[:, 0:cnt, :]
                if ev_eng == "a":
                    P.emit("a", lambda e, d=dst_fn(g0, cnt), s=src3: e.copy(d, s),
                           reads=[bank], writes=[dst_tensor])
                else:
                    P.emit("v", lambda e, d=dst_fn(g0, cnt), s=src3: e.tensor_copy(d, s),
                           reads=[bank], writes=[dst_tensor])

        def prep_wT(wsrc, n_rows, n_cols, wT, wnat):
            nrt = n_rows // 128
            nct_ = n_cols // 128
            wTv = wT[:].rearrange("p (a r) -> p a r", r=n_rows)
            for rt in range(nrt):
                dma("g", wnat[:, 0:n_cols], wsrc[rt * 128:(rt + 1) * 128, :])
                tr_set(nct_, wnat,
                       lambda j, rt=rt: wnat[:, j * 128:(j + 1) * 128],
                       wT,
                       lambda g0, cnt, rt=rt: wTv[:, g0:g0 + cnt,
                                                  rt * 128:(rt + 1) * 128])

        def ln_tile(xa_t, off, sl, lnDump):
            xav = xa_t[:, off:off + C]
            P.emit("a", lambda e: e.activation(lnDump[:], xav, AF.Copy,
                                               accum_out=lnS[:, sl:sl + 1]),
                   reads=[xa_t], writes=[lnDump, lnS])
            P.emit("a", lambda e: e.activation(lnDump[:], xav, AF.Square,
                                               accum_out=lnQ[:, sl:sl + 1]),
                   reads=[xa_t], writes=[lnDump, lnQ])
            P.emit("v", lambda e: e.tensor_scalar(lnM[:, sl:sl + 1],
                                                  lnS[:, sl:sl + 1], INV_C, None,
                                                  ALU.mult),
                   reads=[lnS], writes=[lnM])
            P.emit("v", lambda e: e.tensor_mul(lnT[:, sl:sl + 1], lnM[:, sl:sl + 1],
                                               lnM[:, sl:sl + 1]),
                   reads=[lnM], writes=[lnT])
            P.emit("v", lambda e: e.scalar_tensor_tensor(
                lnQ[:, sl:sl + 1], lnQ[:, sl:sl + 1], INV_C, lnT[:, sl:sl + 1],
                ALU.mult, ALU.subtract), reads=[lnQ, lnT], writes=[lnQ])
            P.emit("a", lambda e: e.activation(lnQ[:, 4 + sl:5 + sl],
                                               lnQ[:, sl:sl + 1], AF.Sqrt,
                                               bias=epsb[:, 0:1]),
                   reads=[lnQ, epsb], writes=[lnQ])
            P.emit("v", lambda e: e.reciprocal(lnR[:, sl:sl + 1],
                                               lnQ[:, 4 + sl:5 + sl]),
                   reads=[lnQ], writes=[lnR])
            P.emit("v", lambda e: e.scalar_tensor_tensor(
                lnT[:, sl:sl + 1], lnM[:, sl:sl + 1], -1.0, lnR[:, sl:sl + 1],
                ALU.mult, ALU.mult), reads=[lnM, lnR], writes=[lnT])
            P.emit("g", lambda e: e.tensor_scalar(
                xav, xav, lnR[:, sl:sl + 1], lnT[:, sl:sl + 1],
                ALU.mult, ALU.add), reads=[xa_t, lnR, lnT], writes=[xa_t])

        def mix(xnf, n, cols, mu_tile, out_bf, tmpf, gper):
            xv = xnf[:].rearrange("p (a r) -> p a r", r=cols)
            ov = out_bf[:].rearrange("p (a r) -> p a r", r=n)
            tv = tmpf[:].rearrange("p (a r) -> p a r", r=n)[:, 0:gper, :]
            for h in range(NCT // gper):
                cs = slice(h * gper, (h + 1) * gper)
                cur = xv[:, cs, 1:n + 1]
                sh = xv[:, cs, 0:n]
                P.emit("v", lambda e, cur=cur, sh=sh, tv=tv: e.tensor_tensor(
                    tv, cur, sh, ALU.subtract),
                    reads=[xnf], writes=[tmpf])
                for j in range(gper):
                    ct = h * gper + j
                    P.emit("g", lambda e, j=j, ct=ct, tv=tv: e.tensor_scalar(
                        tv[:, j, :], tv[:, j, :], mu_tile[:, ct:ct + 1], None,
                        ALU.mult),
                        reads=[tmpf, mu_tile], writes=[tmpf])
                P.emit("v", lambda e, cs=cs, sh=sh, tv=tv: e.tensor_tensor(
                    ov[:, cs, :], tv, sh, ALU.add),
                    reads=[tmpf, xnf], writes=[out_bf])

        def proj(wT3, n_rows, seg0, act3, bank, nw, nk=NCT):
            def f(e):
                ins = None
                for ct in range(nk):
                    ins = e.matmul(bank[:, 0:nw],
                                   wT3[:, ct, seg0 * 128:seg0 * 128 + 128],
                                   act3[:, ct, 0:nw],
                                   start=(ct == 0), stop=(ct == nk - 1))
                return ins
            P.emit("t", f, reads=[wT3, act3], writes=[bank])

        P.emit("v", lambda e: e.memset(ones_f[:], 1.0), writes=[ones_f])
        P.emit("v", lambda e: e.memset(epsb[:], EPS), writes=[epsb])
        P.emit("v", lambda e: e.memset(kapb[:], -KAP), writes=[kapb])
        P.emit("v", lambda e: e.memset(stN[:], 0.0), writes=[stN])
        P.emit("v", lambda e: e.memset(stD[:], 0.0), writes=[stD])
        dma("s", id_f[:], I128in)
        dma("g", id_b[:], I128in)
        dma("g", h_b[:], H128in)
        dma("s", mask_t[:], maskv)
        for (src, dst) in ((mu_r, mur_t), (mu_k, muk_t), (mu_v, muv_t),
                           (wkv_w, ww_t), (wkv_u, wu_t)):
            dma("s", rowp[:], src)

            def fmu(e, rowp=rowp):
                ins = None
                for ct in range(NCT):
                    ins = e.matmul(psT[0][:, ct:ct + 1],
                                   rowp[:, ct * 128:(ct + 1) * 128],
                                   ones_f[:], start=(ct == 0),
                                   stop=(ct == NCT - 1))
                return ins
            P.emit("t", fmu, reads=[rowp, ones_f], writes=[psT[0]])
            P.emit("v", lambda e, dst=dst: e.tensor_copy(dst[:], psT[0][:, 0:NCT]),
                   reads=[psT[0]], writes=[dst])
        P.emit("a", lambda e: e.activation(eu_t[:], ww_t[:], AF.Exp),
               reads=[ww_t], writes=[eu_t])
        P.emit("v", lambda e: e.tensor_copy(dec_t[:], eu_t[:]),
               reads=[eu_t], writes=[dec_t])
        P.emit("a", lambda e: e.activation(dec_t[:], dec_t[:], AF.Exp, scale=-1.0),
               reads=[dec_t], writes=[dec_t])
        P.emit("a", lambda e: e.activation(eu_t[:], wu_t[:], AF.Exp),
               reads=[wu_t], writes=[eu_t])
        P.flush(blk)
        _BAR_()

        NCH = T // CH_B
        BCOLS = CH_B + 1
        with ExitStack() as s1:
            def sb1(name, shape, dt):
                return s1.enter_context(nc.sbuf_tensor(name, shape, dt))
            wT_k = sb1("wT_k", [128, NCT * C], BF16)
            wnat1 = sb1("wnat1", [128, C], BF16)
            lnDump = sb1("lnDump", [128, C], F32)
            xb = [sb1(f"xb{i}", [128, C], F32) for i in range(2)]
            xnf = [sb1(f"xnf{i}", [128, NCT * BCOLS], BF16) for i in range(2)]
            tmp1 = [sb1(f"tmp1_{i}", [128, 4 * CH_B], F32) for i in range(2)]
            xk = [sb1(f"xk{i}", [128, NCT * CH_B], BF16) for i in range(2)]
            ekb = [sb1(f"ekb{i}", [128, CH_B], F32) for i in range(2)]

            def ln_tile_d(xa_t, off, sl):
                ln_tile(xa_t, off, sl, lnDump)

            prep_wT(Wk, C, C, wT_k, wnat1)
            wTk3 = wT_k[:].rearrange("p (a r) -> p a r", r=C)
            xnfv = [t[:].rearrange("p (a r) -> p a r", r=BCOLS) for t in xnf]
            xk3 = [t[:].rearrange("p (a r) -> p a r", r=CH_B) for t in xk]
            P.emit("v", lambda e: e.memset(xnf[0][:], 0.0), writes=[xnf[0]])
            for ch in range(NCH):
                par = ch % 2
                xnfp, xkp, xnfpv = xnf[par], xk[par], xnfv[par]
                for a in range(4):
                    xbp = xb[a % 2]
                    dma("s", xbp[:],
                        xf[ch * CH_B + a * 128: ch * CH_B + (a + 1) * 128, :])
                    ln_tile_d(xbp, 0, a)
                    tr_set(NCT, xbp,
                           lambda j, xbp=xbp: xbp[:, j * 128:(j + 1) * 128],
                           xnfp,
                           lambda g0, cnt, a=a, xnfpv=xnfpv: xnfpv[
                               :, g0:g0 + cnt, 1 + a * 128:1 + (a + 1) * 128],
                           f32=True)
                if ch > 0:
                    P.emit("v", lambda e, xnfpv=xnfpv, ov=xnfv[1 - par]: e.tensor_copy(
                        xnfpv[:, :, 0:1], ov[:, :, CH_B:CH_B + 1]),
                        reads=[xnf[1 - par]], writes=[xnfp])
                if ch == 0:
                    dma("s", xn_sc[:, :, 0:BCOLS].rearrange("a p r -> p a r"),
                        xnfpv)
                else:
                    dma("s", xn_sc[:, :, 1 + ch * CH_B:1 + (ch + 1) * CH_B]
                        .rearrange("a p r -> p a r"), xnfpv[:, :, 1:BCOLS])
                mix(xnfp, CH_B, BCOLS, muk_t, xkp, tmp1[par], 4)
                for ht in range(NCT):
                    bk = psA[ht % 4]
                    proj(wTk3, C, ht, xk3[par], bk, CH_B)
                    eb = ekb[ht % 2]
                    P.emit("a", lambda e, bk=bk, eb=eb: e.activation(
                        eb[:], bk[:, 0:CH_B], AF.Exp, bias=kapb[:, 0:1]),
                        reads=[bk, kapb], writes=[eb])
                    dma("s", ek_sc[ht, :, ch * CH_B:(ch + 1) * CH_B], eb[:])
            P.flush(blk)
        _BAR_()

        with ExitStack() as s2:
            def sb2(name, shape, dt):
                return s2.enter_context(nc.sbuf_tensor(name, shape, dt))
            wT_v = sb2("wT_v", [128, NCT * C], BF16)
            wnat2 = sb2("wnat2", [128, C], BF16)
            xnf2 = [sb2(f"xnf2_{i}", [128, NCT * BCOLS], BF16) for i in range(2)]
            tmp2 = [sb2(f"tmp2_{i}", [128, 4 * CH_B], F32) for i in range(2)]
            xv = [sb2(f"xv{i}", [128, NCT * CH_B], BF16) for i in range(2)]
            ekt = [sb2(f"ekt{i}", [128, CH_B], F32) for i in range(2)]
            vB = [sb2(f"vB{i}", [128, CH_B], F32) for i in range(2)]
            ekvB = [sb2(f"ekvB{i}", [128, CH_B], F32) for i in range(2)]
            Nbuf = [sb2(f"Nbuf{i}", [128, BCOLS], F32) for i in range(2)]
            Dbuf = [sb2(f"Dbuf{i}", [128, BCOLS], F32) for i in range(2)]
            numB = [sb2(f"numB{i}", [128, CH_B], F32) for i in range(2)]
            denB = [sb2(f"denB{i}", [128, CH_B], F32) for i in range(2)]
            yB = [sb2(f"yB{i}", [128, CH_B], F32) for i in range(2)]

            prep_wT(Wv, C, C, wT_v, wnat2)
            wTv3 = wT_v[:].rearrange("p (a r) -> p a r", r=C)
            xnf2v = [t[:].rearrange("p (a r) -> p a r", r=BCOLS) for t in xnf2]
            xv3 = [t[:].rearrange("p (a r) -> p a r", r=CH_B) for t in xv]
            for ch in range(NCH):
                par = ch % 2
                own = ch >= NCH // 2
                dma("s", xnf2v[par],
                    xn_sc[:, :, ch * CH_B: ch * CH_B + BCOLS].rearrange("a p r -> p a r"))
                mix(xnf2[par], CH_B, BCOLS, muv_t, xv[par], tmp2[par], 4)
                if ch == NCH // 2:
                    P.emit("v", lambda e: e.tensor_scalar(
                        stN[:], stN[:], mask_t[:, 0:1], None, ALU.mult),
                        reads=[stN, mask_t], writes=[stN])
                    P.emit("v", lambda e: e.tensor_scalar(
                        stD[:], stD[:], mask_t[:, 0:1], None, ALU.mult),
                        reads=[stD, mask_t], writes=[stD])
                for ht in range(NCT):
                    p2 = ht % 2
                    ek_ = ekt[p2]
                    dma("s", ek_[:], ek_sc[ht, :, ch * CH_B:(ch + 1) * CH_B])
                    bv = psA[ht % 4]
                    proj(wTv3, C, ht, xv3[par], bv, CH_B)
                    vb = vB[p2]
                    P.emit("a", lambda e, bv=bv, vb=vb: e.copy(vb[:], bv[:, 0:CH_B]),
                           reads=[bv], writes=[vb])
                    ekv_ = ekvB[p2]
                    P.emit("v", lambda e, ek_=ek_, vb=vb, ekv_=ekv_: e.tensor_mul(
                        ekv_[:], ek_[:], vb[:]), reads=[ek_, vb], writes=[ekv_])
                    decb = dec_t[:, ht:ht + 1].broadcast_to([128, CH_B])
                    Nb, Db = Nbuf[p2], Dbuf[p2]
                    if own:
                        P.emit("a", lambda e, ht=ht, Nb=Nb: e.copy(
                            Nb[:, 0:1], stN[:, ht:ht + 1]), reads=[stN], writes=[Nb])
                        P.emit("a", lambda e, ht=ht, Db=Db: e.copy(
                            Db[:, 0:1], stD[:, ht:ht + 1]), reads=[stD], writes=[Db])
                    P.emit("v", lambda e, ht=ht, decb=decb, ekv_=ekv_, Nb=Nb:
                           e.tensor_tensor_scan(
                               Nb[:, 1:BCOLS], decb, ekv_[:], stN[:, ht:ht + 1],
                               ALU.mult, ALU.add),
                           reads=[ekv_, dec_t, stN], writes=[Nb])
                    P.emit("a", lambda e, ht=ht, Nb=Nb: e.copy(
                        stN[:, ht:ht + 1], Nb[:, CH_B:CH_B + 1]),
                        reads=[Nb], writes=[stN])
                    P.emit("v", lambda e, ht=ht, decb=decb, ek_=ek_, Db=Db:
                           e.tensor_tensor_scan(
                               Db[:, 1:BCOLS], decb, ek_[:], stD[:, ht:ht + 1],
                               ALU.mult, ALU.add),
                           reads=[ek_, dec_t, stD], writes=[Db])
                    P.emit("a", lambda e, ht=ht, Db=Db: e.copy(
                        stD[:, ht:ht + 1], Db[:, CH_B:CH_B + 1]),
                        reads=[Db], writes=[stD])
                    if own:
                        nm, dn, yb = numB[p2], denB[p2], yB[p2]
                        P.emit("v", lambda e, ht=ht, ekv_=ekv_, Nb=Nb, nm=nm:
                               e.scalar_tensor_tensor(
                                   nm[:], ekv_[:], eu_t[:, ht:ht + 1],
                                   Nb[:, 0:CH_B], ALU.mult, ALU.add),
                               reads=[ekv_, eu_t, Nb], writes=[nm])
                        P.emit("v", lambda e, ht=ht, ek_=ek_, Db=Db, dn=dn:
                               e.scalar_tensor_tensor(
                                   dn[:], ek_[:], eu_t[:, ht:ht + 1],
                                   Db[:, 0:CH_B], ALU.mult, ALU.add),
                               reads=[ek_, eu_t, Db], writes=[dn])
                        P.emit("a", lambda e, dn=dn: e.activation(
                            dn[:], dn[:], AF.Ln), reads=[dn], writes=[dn])
                        P.emit("a", lambda e, dn=dn: e.activation(
                            dn[:], dn[:], AF.Exp, scale=-1.0),
                            reads=[dn], writes=[dn])
                        P.emit("v", lambda e, nm=nm, dn=dn, yb=yb: e.tensor_mul(
                            yb[:], nm[:], dn[:]), reads=[nm, dn], writes=[yb])
                        dma("s", y_sc[ht, :, (ch - NCH // 2) * CH_B:
                                      (ch - NCH // 2 + 1) * CH_B], yb[:])
            P.flush(blk)
        _BAR_()

        with ExitStack() as s3:
            def sb3(name, shape, dt):
                return s3.enter_context(nc.sbuf_tensor(name, shape, dt))
            wT_r = sb3("wT_r", [128, NCT * C], BF16)
            wnat3 = sb3("wnat3", [128, C], BF16)
            xnf3 = [sb3(f"xnf3_{i}", [128, NCT * BCOLS], BF16) for i in range(2)]
            tmp3 = [sb3(f"tmp3_{i}", [128, 4 * CH_B], F32) for i in range(2)]
            xr = [sb3(f"xr{i}", [128, NCT * CH_B], BF16) for i in range(2)]
            srB = [sb3(f"srB{i}", [128, CH_B], F32) for i in range(2)]
            yt = [sb3(f"yt{i}", [128, CH_B], F32) for i in range(2)]
            zt = [sb3(f"zt{i}", [128, CH_B], BF16) for i in range(2)]

            prep_wT(Wr, C, C, wT_r, wnat3)
            wTr3 = wT_r[:].rearrange("p (a r) -> p a r", r=C)
            xnf3v = [t[:].rearrange("p (a r) -> p a r", r=BCOLS) for t in xnf3]
            xr3 = [t[:].rearrange("p (a r) -> p a r", r=CH_B) for t in xr]
            for ch8 in range(NCH_B):
                ch = NCH // 2 + ch8
                par = ch8 % 2
                dma("s", xnf3v[par],
                    xn_sc[:, :, ch * CH_B: ch * CH_B + BCOLS].rearrange("a p r -> p a r"))
                mix(xnf3[par], CH_B, BCOLS, mur_t, xr[par], tmp3[par], 4)
                for ht in range(NCT):
                    p2 = ht % 2
                    ytb = yt[p2]
                    dma("s", ytb[:], y_sc[ht, :, ch8 * CH_B:(ch8 + 1) * CH_B])
                    br = psA[ht % 4]
                    proj(wTr3, C, ht, xr3[par], br, CH_B)
                    sr = srB[p2]
                    P.emit("a", lambda e, br=br, sr=sr: e.activation(
                        sr[:], br[:, 0:CH_B], AF.Sigmoid), reads=[br], writes=[sr])
                    ztb = zt[p2]
                    P.emit("v", lambda e, ztb=ztb, ytb=ytb, sr=sr: e.tensor_mul(
                        ztb[:], sr[:], ytb[:]), reads=[sr, ytb], writes=[ztb])
                    dma("s", z_sc[ht, :, ch8 * CH_B:(ch8 + 1) * CH_B], ztb[:])
            P.flush(blk)
        _BAR_()

        with ExitStack() as s4:
            def sb4(name, shape, dt):
                return s4.enter_context(nc.sbuf_tensor(name, shape, dt))
            wT_o = sb4("wT_o", [128, NCT * C], BF16)
            wnat4 = sb4("wnat4", [128, C], BF16)
            ztl = [sb4(f"ztl{i}", [128, NCT * TB], BF16) for i in range(2)]
            atl = [sb4(f"atl{i}", [128, TB], F32) for i in range(2)]

            prep_wT(Wo, C, C, wT_o, wnat4)
            wTo3 = wT_o[:].rearrange("p (a r) -> p a r", r=C)
            for tb in range(NTB):
                zb = ztl[tb % 2]
                zb3 = zb[:].rearrange("p (a r) -> p a r", r=TB)
                dma("s", zb3,
                    z_sc[:, :, tb * TB:(tb + 1) * TB].rearrange("a p r -> p a r"))
                for dt_i in range(NCT):
                    bo = psA[dt_i % 4]
                    proj(wTo3, C, dt_i, zb3, bo, TB)
                    ab = atl[dt_i % 2]
                    P.emit("v", lambda e, bo=bo, ab=ab: e.tensor_copy(
                        ab[:], bo[:, 0:TB]), reads=[bo], writes=[ab])
                    dma("s", at_sc[dt_i, :, tb * TB:(tb + 1) * TB], ab[:])
            P.flush(blk)
        _BAR_()

        with ExitStack() as s5:
            def sb5(name, shape, dt):
                return s5.enter_context(nc.sbuf_tensor(name, shape, dt))
            xnC = sb5("xnC", [128, NCT * TH], BF16)
            w1cT = [sb5(f"w1cT{i}", [128, NCT * 256], BF16) for i in range(2)]
            w3cT = [sb5(f"w3cT{i}", [128, NCT * 256], BF16) for i in range(2)]
            wnat5 = [sb5(f"wnat5_{i}", [128, 2 * C], BF16) for i in range(2)]
            sgC = [sb5(f"sgC{i}", [128, TB], BF16) for i in range(2)]
            lC = [sb5(f"lC{i}", [128, TB], BF16) for i in range(2)]
            hC = [sb5(f"hC{i}", [128, TB], BF16) for i in range(2)]

            xnCv = xnC[:].rearrange("p (a r) -> p a r", r=TH)
            for q4 in range(4):
                dma("s", xnCv[:, q4 * 4:(q4 + 1) * 4, :],
                    xn_sc[q4 * 4:(q4 + 1) * 4, :, 1 + TH:T + 1].rearrange("a p r -> p a r"))

            w1cT3 = [t[:].rearrange("p (a r) -> p a r", r=256) for t in w1cT]
            w3cT3 = [t[:].rearrange("p (a r) -> p a r", r=256) for t in w3cT]

            def prep_w13(wsrc, wT, wTv, fc, par):
                wn = wnat5[par]
                dma("g", wn[:].rearrange("p (a c) -> p a c", c=C),
                    wsrc[fc * 256: (fc + 1) * 256, :]
                    .rearrange("(a p) c -> p a c", p=128))
                for a in range(2):
                    r0 = a * 128
                    tr_set(NCT, wn,
                           lambda j, a=a, wn=wn: wn[:, a * C + j * 128:
                                                    a * C + (j + 1) * 128],
                           wT,
                           lambda g0, cnt, r0=r0, wTv=wTv: wTv[:, g0:g0 + cnt,
                                                               r0:r0 + 128])

            for fc in range(FF // 256):
                par = fc % 2
                prep_w13(W1, w1cT[par], w1cT3[par], fc, par)
                prep_w13(W3, w3cT[par], w3cT3[par], fc, par)
                for tb in range(NTB):
                    act3 = xnCv[:, :, tb * TB:(tb + 1) * TB]
                    for ft in range(2):
                        bg = psA[2 * (ft % 2)]
                        bl = psA[2 * (ft % 2) + 1]
                        proj(w1cT3[par], 256, ft, act3, bg, TB)
                        proj(w3cT3[par], 256, ft, act3, bl, TB)
                        sg = sgC[ft % 2]
                        lb = lC[ft % 2]
                        hb = hC[ft % 2]
                        P.emit("a", lambda e, bg=bg, sg=sg: e.activation(
                            sg[:], bg[:, 0:TB], AF.Silu), reads=[bg], writes=[sg])
                        P.emit("v", lambda e, bl=bl, lb=lb: e.tensor_copy(
                            lb[:], bl[:, 0:TB]), reads=[bl], writes=[lb])
                        P.emit("v", lambda e, hb=hb, sg=sg, lb=lb: e.tensor_mul(
                            hb[:], sg[:], lb[:]), reads=[sg, lb], writes=[hb])
                        dma("s", hid_sc[fc * 2 + ft, :, tb * TB:(tb + 1) * TB],
                            hb[:])
            P.flush(blk)
        _BAR_()

        with ExitStack() as s6:
            def sb6(name, shape, dt):
                return s6.enter_context(nc.sbuf_tensor(name, shape, dt))
            w2Tq = sb6("w2Tq", [128, NFT * 512], BF16)
            w2nat = sb6("w2nat", [128, FF // 2], BF16)
            big = sb6("big", [128, NFT * 512], BF16)
            at4 = sb6("at4", [128, 4 * TB], F32)
            xsl = sb6("xsl", [128, 4 * TB], F32)
            fa = sb6("fa", [128, 4 * TB], F32)
            outt = sb6("outt", [128, 4 * TB], F32)

            w2Tqv = w2Tq[:].rearrange("p (a r) -> p a r", r=512)
            bigv = big[:].rearrange("p (a r) -> p a r", r=512)
            for q in range(4):
                for sub in range(4):
                    r0 = q * 512 + sub * 128
                    for half in range(2):
                        dma("g", w2nat[:], W2[r0:r0 + 128,
                                              half * (FF // 2):(half + 1) * (FF // 2)])
                        tr_set(32, w2nat,
                               lambda j: w2nat[:, j * 128:(j + 1) * 128],
                               w2Tq,
                               lambda g0, cnt, half=half, sub=sub: w2Tqv[
                                   :, half * 32 + g0: half * 32 + g0 + cnt,
                                   sub * 128:(sub + 1) * 128])
                for ot in range(NFT):
                    bank = psA[ot % 2]
                    P.emit("t", lambda e, ot=ot, bank=bank: e.matmul(
                        bank[:], h_b[:], w2Tq[:, ot * 512:(ot + 1) * 512],
                        start=True, stop=True),
                        reads=[h_b, w2Tq], writes=[bank])
                    P.emit("a", lambda e, ot=ot, bank=bank: e.copy(
                        w2Tq[:, ot * 512:(ot + 1) * 512], bank[:]),
                        reads=[bank], writes=[w2Tq])
                for lev in range(6):
                    gN = NFT >> (lev + 1)
                    iN = 1 << lev
                    view = w2Tq[:].rearrange("p (g two i d) -> p g two i d",
                                             g=gN, two=2, i=iN)
                    av = view[:, :, 0, :, :]
                    bv = view[:, :, 1, :, :]
                    tv = big[:].rearrange("p (g i d) -> p g i d",
                                          g=gN, i=iN)[:, :, :, 0:512]
                    P.emit("v", lambda e, tv=tv, av=av, bv=bv: e.tensor_tensor(
                        tv, av, bv, ALU.subtract), reads=[w2Tq], writes=[big])
                    P.emit("v", lambda e, av=av, bv=bv: e.tensor_tensor(
                        av, av, bv, ALU.add), reads=[w2Tq], writes=[w2Tq])
                    P.emit("a", lambda e, bv=bv, tv=tv: e.copy(bv, tv),
                           reads=[big], writes=[w2Tq])
                for tb in range(NTB):
                    dma("s", bigv[:, 0:32, :],
                        hid_sc[0:32, :, tb * TB:(tb + 1) * TB].rearrange("a p r -> p a r"))
                    dma("s", bigv[:, 32:64, :],
                        hid_sc[32:64, :, tb * TB:(tb + 1) * TB].rearrange("a p r -> p a r"))
                    dma("s", at4[:].rearrange("p (a r) -> p a r", r=TB),
                        at_sc[q * 4:(q + 1) * 4, :, tb * TB:(tb + 1) * TB]
                        .rearrange("a p r -> p a r"))
                    dma("s", xsl[:].rearrange("p (a r) -> p a r", r=TB),
                        x[tb * TB:(tb + 1) * TB, q * 512:(q + 1) * 512]
                        .rearrange("(a p) d -> p a d", p=128))
                    for dt_i in range(4):
                        bank = psA[dt_i % 4]

                        def fmm(e, dt_i=dt_i, bank=bank):
                            ins = None
                            for ot in range(NFT):
                                ins = e.matmul(
                                    bank[:],
                                    w2Tq[:, ot * 512 + dt_i * 128:
                                         ot * 512 + dt_i * 128 + 128],
                                    big[:, ot * 512:(ot + 1) * 512],
                                    start=(ot == 0), stop=(ot == NFT - 1))
                            return ins
                        P.emit("t", fmm, reads=[w2Tq, big], writes=[bank])
                        P.emit("v", lambda e, dt_i=dt_i, bank=bank: e.scalar_tensor_tensor(
                            fa[:, dt_i * TB:(dt_i + 1) * TB], bank[:], RSQ,
                            at4[:, dt_i * TB:(dt_i + 1) * TB], ALU.mult, ALU.add),
                            reads=[bank, at4], writes=[fa])
                    for ts_i in range(4):
                        bank = psT[ts_i % 2]

                        def ftr(e, ts_i=ts_i, bank=bank):
                            ins = None
                            for dt_i in range(4):
                                ins = e.transpose(
                                    bank[:, dt_i * 128:(dt_i + 1) * 128],
                                    fa[:, dt_i * TB + ts_i * 128:
                                       dt_i * TB + ts_i * 128 + 128],
                                    id_f[:])
                            return ins
                        P.emit("t", ftr, reads=[fa, id_f], writes=[bank])
                        P.emit("v", lambda e, ts_i=ts_i, bank=bank: e.tensor_add(
                            outt[:, ts_i * 512:(ts_i + 1) * 512], bank[:],
                            xsl[:, ts_i * 512:(ts_i + 1) * 512]),
                            reads=[bank, xsl], writes=[outt])
                    dma("s", yout[tb * TB:(tb + 1) * TB, q * 512:(q + 1) * 512]
                        .rearrange("(a p) d -> p a d", p=128),
                        outt[:].rearrange("p (a d) -> p a d", d=512))
            for sn in sorted(P.cnt):
                if not sn.startswith("dma_"):
                    continue
                v = P.cnt[sn]
                h = P.sems[sn]
                P.q["s"].append(lambda e, h=h, v=v: e.wait_ge(h, v))
            P.flush(blk)
        _BAR_()

    return nc


_NC_CACHE = {}


def kernel(L0_attn_x, L0_attn_mu_r, L0_attn_mu_k, L0_attn_mu_v, L0_attn_wkv_w,
           L0_attn_wkv_u, L0_attn_Wr, L0_attn_Wk, L0_attn_Wv, L0_attn_Wo,
           L0_ffn_W1, L0_ffn_W2, L0_ffn_W3):
    x_full = np.ascontiguousarray(np.asarray(L0_attn_x, dtype=np.float32))
    w = {k: np.ascontiguousarray(np.asarray(v, dtype=np.float32)) for k, v in dict(
        Wr=L0_attn_Wr, Wk=L0_attn_Wk, Wv=L0_attn_Wv, Wo=L0_attn_Wo,
        W1=L0_ffn_W1, W2=L0_ffn_W2, W3=L0_ffn_W3).items()}
    vecs = {k: np.ascontiguousarray(np.asarray(v, dtype=np.float32)).reshape(1, C)
            for k, v in dict(mu_r=L0_attn_mu_r, mu_k=L0_attn_mu_k,
                             mu_v=L0_attn_mu_v, wkv_w=L0_attn_wkv_w,
                             wkv_u=L0_attn_wkv_u).items()}
    H = hadamard(128)
    I = np.eye(128, dtype=np.float32)

    if "nc" not in _NC_CACHE:
        _NC_CACHE["nc"] = build_nc()
    nc = _NC_CACHE["nc"]

    in_maps = []
    zpref = np.zeros((TH, C), np.float32)
    for c in range(8):
        b, s = c >> 1, c & 1
        t0 = s * TH
        own = np.ascontiguousarray(x_full[b, t0:t0 + TH])
        pref = np.ascontiguousarray(x_full[b, 0:TH]) if s else zpref
        m = dict(x=own,
                 xf=np.ascontiguousarray(np.concatenate([pref, own], axis=0)),
                 maskv=np.full((128, 1), float(s), np.float32),
                 H128in=H, I128in=I, **w, **vecs)
        in_maps.append(m)

    res = run_bass_kernel_spmd(nc, in_maps, core_ids=list(range(8)))
    out = np.empty((B, T, C), dtype=np.float32)
    for c in range(8):
        b, s = c >> 1, c & 1
        out[b, s * TH:(s + 1) * TH] = res.results[c]["y"]
    return out


# version 05
# speedup vs baseline: 1.0807x; 1.0807x over previous
import math
from contextlib import ExitStack

import numpy as np
import ml_dtypes
import concourse.bass as bass
import concourse.mybir as mybir
from concourse.bass_utils import run_bass_kernel_spmd

F32 = mybir.dt.float32
BF16 = mybir.dt.bfloat16
AF = mybir.ActivationFunctionType
ALU = mybir.AluOpType

B, T, C, FF = 4, 8192, 2048, 8192
TH = T // 2
NCT = C // 128
NFT = FF // 128
KAP = 12.0
RSQ = 1.0 / math.sqrt(FF)
EPS = 1e-5
INV_C = 1.0 / C

CH_A = 256
NCH_A = TH // CH_A
CH_B = 512
NCH_B = TH // CH_B
TB = 512
NTB = TH // TB

ENGS = ("s", "g", "t", "v", "a")


class Prog:

    def __init__(self, nc, sem_alloc):
        self.nc = nc
        self.q = {e: [] for e in ENGS}
        self.sem_alloc = sem_alloc
        self.sems = {}
        self.cnt = {}
        for e in ENGS:
            self.sems[e] = sem_alloc("sem_" + e)
            self.cnt[e] = 0
        self.writer = {}
        self.readers = {}
        self.waited = {e: {} for e in ENGS}

    @staticmethod
    def _key(ap):
        t = getattr(ap, "tensor", ap)
        return getattr(t, "name", str(t))

    def buf_sem(self, name):
        sn = "dma_" + name
        if sn not in self.sems:
            self.sems[sn] = self.sem_alloc(sn)
            self.cnt[sn] = 0
        return sn

    def emit(self, eng, f, reads=(), writes=(), dma=False, dma_buf=None):
        waits = {}

        def need(sem_name, val, src_dma, src_eng):
            if self.waited[eng].get(sem_name, -1) >= val:
                return
            waits[sem_name] = max(waits.get(sem_name, 0), val)

        for r in reads:
            w = self.writer.get(self._key(r))
            if w:
                need(*w)
        for wr in writes:
            k = self._key(wr)
            w = self.writer.get(k)
            if w:
                need(*w)
            for rd in self.readers.get(k, ()):
                need(*rd)
        for sn, v in waits.items():
            self.waited[eng][sn] = max(self.waited[eng].get(sn, -1), v)
        if dma:
            assert dma_buf is not None
            sn = self.buf_sem(self._key(dma_buf))
            d = 16
        else:
            sn = eng
            d = 1
        self.cnt[sn] += d
        val = self.cnt[sn]
        my_sem = self.sems[sn]
        wlist = [(self.sems[s2], v) for s2, v in sorted(waits.items())]

        def g(e, f=f, wlist=wlist, my_sem=my_sem, d=d):
            for (h, v) in wlist:
                e.wait_ge(h, v)
            ins = f(e)
            if ins is not None:
                ins.then_inc(my_sem, d)

        self.q[eng].append(g)
        mark = (sn, val, dma, eng)
        for wr in writes:
            k = self._key(wr)
            self.writer[k] = mark
            self.readers[k] = []
        for r in reads:
            self.readers.setdefault(self._key(r), []).append(mark)
        return mark

    def barrier(self):
        snap = dict(self.cnt)
        for eng in ENGS:
            for sn in sorted(snap):
                v = snap[sn]
                if v <= 0 or self.waited[eng].get(sn, -1) >= v:
                    continue
                h = self.sems[sn]
                self.q[eng].append(lambda e, h=h, v=v: e.wait_ge(h, v))
                self.waited[eng][sn] = v

    def flush(self, blk):
        qs = {e: self.q[e] for e in ENGS}
        self.q = {e: [] for e in ENGS}
        if qs["s"]:
            fs = qs["s"]

            @blk.sync
            def _(e, fs=fs):
                for f in fs:
                    f(e)
        if qs["g"]:
            fg = qs["g"]

            @blk.gpsimd
            def _(e, fg=fg):
                for f in fg:
                    f(e)
        if qs["t"]:
            ft = qs["t"]

            @blk.tensor
            def _(e, ft=ft):
                for f in ft:
                    f(e)
        if qs["v"]:
            fv = qs["v"]

            @blk.vector
            def _(e, fv=fv):
                for f in fv:
                    f(e)
        if qs["a"]:
            fa = qs["a"]

            @blk.scalar
            def _(e, fa=fa):
                for f in fa:
                    f(e)


def hadamard(n):
    h = np.array([[1.0]], dtype=np.float32)
    while h.shape[0] < n:
        h = np.block([[h, h], [h, -h]]).astype(np.float32)
    return h


def build_nc():
    nc = bass.Bass("TRN2", target_bir_lowering=False)

    def din(name, shape):
        return nc.dram_tensor(name, shape, F32, kind="ExternalInput").ap()

    x = din("x", [TH, C])
    xf = din("xf", [T, C])
    maskv = din("maskv", [128, 1])
    Wr = din("Wr", [C, C]); Wk = din("Wk", [C, C]); Wv = din("Wv", [C, C])
    Wo = din("Wo", [C, C])
    W1 = din("W1", [FF, C]); W3 = din("W3", [FF, C]); W2 = din("W2", [C, FF])
    mu_r = din("mu_r", [1, C]); mu_k = din("mu_k", [1, C]); mu_v = din("mu_v", [1, C])
    wkv_w = din("wkv_w", [1, C]); wkv_u = din("wkv_u", [1, C])
    H128in = din("H128in", [128, 128])
    I128in = din("I128in", [128, 128])

    yout = nc.dram_tensor("y", [TH, C], F32, kind="ExternalOutput").ap()

    xn_sc = nc.dram_tensor("xn_sc", [NCT, 128, T + 1], BF16, kind="Internal").ap()
    ek_sc = nc.dram_tensor("ek_sc", [NCT, 128, T], F32, kind="Internal").ap()
    y_sc = nc.dram_tensor("y_sc", [NCT, 128, TH], F32, kind="Internal").ap()
    z_sc = nc.dram_tensor("z_sc", [NCT, 128, TH], BF16, kind="Internal").ap()
    hid_sc = nc.dram_tensor("hid_sc", [NFT, 128, TH], BF16, kind="Internal").ap()
    at_sc = nc.dram_tensor("at_sc", [NCT, 128, TH], F32, kind="Internal").ap()

    with ExitStack() as ctx:
        def sb(name, shape, dt):
            return ctx.enter_context(nc.sbuf_tensor(name, shape, dt))

        def psum(name, shape, dt):
            return ctx.enter_context(nc.psum_tensor(name, shape, dt))

        def sem_alloc(name):
            return ctx.enter_context(nc.semaphore(name))
        P = Prog(nc, sem_alloc)

        def _BAR_():
            P.barrier()

        ones_f = sb("ones_f", [1, 1], F32)
        id_f = sb("id_f", [128, 128], F32)
        id_b = sb("id_b", [128, 128], BF16)
        h_b = sb("h_b", [128, 128], BF16)
        rowp = sb("rowp", [1, C], F32)
        rown = sb("rown", [1, C], F32)
        mur_t = sb("mur_t", [128, NCT], F32)
        muk_t = sb("muk_t", [128, NCT], F32)
        muv_t = sb("muv_t", [128, NCT], F32)
        ww_t = sb("ww_t", [128, NCT], F32)
        wu_t = sb("wu_t", [128, NCT], F32)
        dec_t = sb("dec_t", [128, NCT], F32)
        eu_t = sb("eu_t", [128, NCT], F32)
        mask_t = sb("mask_t", [128, 1], F32)
        stN = sb("stN", [128, NCT], F32)
        stD = sb("stD", [128, NCT], F32)
        lnS = sb("lnS", [128, 8], F32)
        lnQ = sb("lnQ", [128, 8], F32)
        lnM = sb("lnM", [128, 8], F32)
        lnR = sb("lnR", [128, 8], F32)
        lnT = sb("lnT", [128, 8], F32)
        epsb = sb("epsb", [128, 1], F32)
        kapb = sb("kapb", [128, 1], F32)

        psA = [psum(f"psA{i}", [128, 512], F32) for i in range(4)]
        psT = [psum(f"psT{i}", [128, 512], F32) for i in range(2)]
        psB = [psum(f"psB{i}", [128, 1024], BF16) for i in range(2)]

        blk = ctx.enter_context(nc.Block())

        def dma(eng, out, in_, reads=(), writes=()):
            t_out = getattr(out, "tensor", out)
            buf = out if "SB" in type(t_out).__name__ else in_
            P.emit(eng, lambda e, o=out, i=in_: e.dma_start(out=o, in_=i),
                   reads=tuple(reads) + (in_,), writes=tuple(writes) + (out,),
                   dma=True, dma_buf=buf)

        def tr_set(n, src_tensor, src_fn, dst_tensor, dst_fn, f32=False):
            per = 4 if f32 else 8
            banks = psT if f32 else psB
            ident = id_f if f32 else id_b
            ev_eng = "v"
            for g0 in range(0, n, per):
                cnt = min(per, n - g0)
                bank = banks[(g0 // per) % 2]

                def ftr(e, g0=g0, cnt=cnt, bank=bank):
                    ins = None
                    for j in range(cnt):
                        ins = e.transpose(bank[:, j * 128:(j + 1) * 128],
                                          src_fn(g0 + j), ident[:])
                    return ins
                P.emit("t", ftr, reads=[src_tensor, ident], writes=[bank])
                src3 = bank[:].rearrange("p (g r) -> p g r", r=128)[:, 0:cnt, :]
                if ev_eng == "a":
                    P.emit("a", lambda e, d=dst_fn(g0, cnt), s=src3: e.copy(d, s),
                           reads=[bank], writes=[dst_tensor])
                else:
                    P.emit("v", lambda e, d=dst_fn(g0, cnt), s=src3: e.tensor_copy(d, s),
                           reads=[bank], writes=[dst_tensor])

        def prep_wT(wsrc, n_rows, n_cols, wT, wnat):
            nrt = n_rows // 128
            nct_ = n_cols // 128
            wTv = wT[:].rearrange("p (a r) -> p a r", r=n_rows)
            for rt in range(nrt):
                dma("g", wnat[:, 0:n_cols], wsrc[rt * 128:(rt + 1) * 128, :])
                tr_set(nct_, wnat,
                       lambda j, rt=rt: wnat[:, j * 128:(j + 1) * 128],
                       wT,
                       lambda g0, cnt, rt=rt: wTv[:, g0:g0 + cnt,
                                                  rt * 128:(rt + 1) * 128])

        def ln_tile(xa_t, off, sl, lnDump):
            xav = xa_t[:, off:off + C]
            P.emit("a", lambda e: e.activation(lnDump[:], xav, AF.Copy,
                                               accum_out=lnS[:, sl:sl + 1]),
                   reads=[xa_t], writes=[lnDump, lnS])
            P.emit("a", lambda e: e.activation(lnDump[:], xav, AF.Square,
                                               accum_out=lnQ[:, sl:sl + 1]),
                   reads=[xa_t], writes=[lnDump, lnQ])
            P.emit("v", lambda e: e.tensor_scalar(lnM[:, sl:sl + 1],
                                                  lnS[:, sl:sl + 1], INV_C, None,
                                                  ALU.mult),
                   reads=[lnS], writes=[lnM])
            P.emit("v", lambda e: e.tensor_mul(lnT[:, sl:sl + 1], lnM[:, sl:sl + 1],
                                               lnM[:, sl:sl + 1]),
                   reads=[lnM], writes=[lnT])
            P.emit("v", lambda e: e.scalar_tensor_tensor(
                lnQ[:, sl:sl + 1], lnQ[:, sl:sl + 1], INV_C, lnT[:, sl:sl + 1],
                ALU.mult, ALU.subtract), reads=[lnQ, lnT], writes=[lnQ])
            P.emit("a", lambda e: e.activation(lnQ[:, 4 + sl:5 + sl],
                                               lnQ[:, sl:sl + 1], AF.Sqrt,
                                               bias=epsb[:, 0:1]),
                   reads=[lnQ, epsb], writes=[lnQ])
            P.emit("v", lambda e: e.reciprocal(lnR[:, sl:sl + 1],
                                               lnQ[:, 4 + sl:5 + sl]),
                   reads=[lnQ], writes=[lnR])
            P.emit("v", lambda e: e.scalar_tensor_tensor(
                lnT[:, sl:sl + 1], lnM[:, sl:sl + 1], -1.0, lnR[:, sl:sl + 1],
                ALU.mult, ALU.mult), reads=[lnM, lnR], writes=[lnT])
            P.emit("g", lambda e: e.tensor_scalar(
                xav, xav, lnR[:, sl:sl + 1], lnT[:, sl:sl + 1],
                ALU.mult, ALU.add), reads=[xa_t, lnR, lnT], writes=[xa_t])

        def mix(xnf, n, cols, mu_tile, out_bf, tmpf, gper):
            xv = xnf[:].rearrange("p (a r) -> p a r", r=cols)
            ov = out_bf[:].rearrange("p (a r) -> p a r", r=n)
            tv = tmpf[:].rearrange("p (a r) -> p a r", r=n)[:, 0:gper, :]
            for h in range(NCT // gper):
                cs = slice(h * gper, (h + 1) * gper)
                cur = xv[:, cs, 1:n + 1]
                sh = xv[:, cs, 0:n]
                P.emit("g", lambda e, cur=cur, sh=sh, tv=tv: e.tensor_tensor(
                    tv, cur, sh, ALU.subtract),
                    reads=[xnf], writes=[tmpf])
                for j in range(gper):
                    ct = h * gper + j
                    P.emit("g", lambda e, j=j, ct=ct, tv=tv: e.tensor_scalar(
                        tv[:, j, :], tv[:, j, :], mu_tile[:, ct:ct + 1], None,
                        ALU.mult),
                        reads=[tmpf, mu_tile], writes=[tmpf])
                P.emit("g", lambda e, cs=cs, sh=sh, tv=tv: e.tensor_tensor(
                    ov[:, cs, :], tv, sh, ALU.add),
                    reads=[tmpf, xnf], writes=[out_bf])

        def proj(wT3, n_rows, seg0, act3, bank, nw, nk=NCT):
            def f(e):
                ins = None
                for ct in range(nk):
                    ins = e.matmul(bank[:, 0:nw],
                                   wT3[:, ct, seg0 * 128:seg0 * 128 + 128],
                                   act3[:, ct, 0:nw],
                                   start=(ct == 0), stop=(ct == nk - 1))
                return ins
            P.emit("t", f, reads=[wT3, act3], writes=[bank])

        P.emit("v", lambda e: e.memset(ones_f[:], 1.0), writes=[ones_f])
        P.emit("v", lambda e: e.memset(epsb[:], EPS), writes=[epsb])
        P.emit("v", lambda e: e.memset(kapb[:], -KAP), writes=[kapb])
        P.emit("v", lambda e: e.memset(stN[:], 0.0), writes=[stN])
        P.emit("v", lambda e: e.memset(stD[:], 0.0), writes=[stD])
        dma("s", id_f[:], I128in)
        dma("g", id_b[:], I128in)
        dma("g", h_b[:], H128in)
        dma("s", mask_t[:], maskv)
        for (src, dst) in ((mu_r, mur_t), (mu_k, muk_t), (mu_v, muv_t),
                           (wkv_w, ww_t), (wkv_u, wu_t)):
            dma("s", rowp[:], src)

            def fmu(e, rowp=rowp):
                ins = None
                for ct in range(NCT):
                    ins = e.matmul(psT[0][:, ct:ct + 1],
                                   rowp[:, ct * 128:(ct + 1) * 128],
                                   ones_f[:], start=(ct == 0),
                                   stop=(ct == NCT - 1))
                return ins
            P.emit("t", fmu, reads=[rowp, ones_f], writes=[psT[0]])
            P.emit("v", lambda e, dst=dst: e.tensor_copy(dst[:], psT[0][:, 0:NCT]),
                   reads=[psT[0]], writes=[dst])
        P.emit("a", lambda e: e.activation(eu_t[:], ww_t[:], AF.Exp),
               reads=[ww_t], writes=[eu_t])
        P.emit("v", lambda e: e.tensor_copy(dec_t[:], eu_t[:]),
               reads=[eu_t], writes=[dec_t])
        P.emit("a", lambda e: e.activation(dec_t[:], dec_t[:], AF.Exp, scale=-1.0),
               reads=[dec_t], writes=[dec_t])
        P.emit("a", lambda e: e.activation(eu_t[:], wu_t[:], AF.Exp),
               reads=[wu_t], writes=[eu_t])
        P.flush(blk)
        _BAR_()

        NCH = T // CH_B
        BCOLS = CH_B + 1
        with ExitStack() as s1:
            def sb1(name, shape, dt):
                return s1.enter_context(nc.sbuf_tensor(name, shape, dt))
            wT_k = sb1("wT_k", [128, NCT * C], BF16)
            wnat1 = sb1("wnat1", [128, C], BF16)
            lnDump = sb1("lnDump", [128, C], F32)
            xb = [sb1(f"xb{i}", [128, C], F32) for i in range(2)]
            xnf = [sb1(f"xnf{i}", [128, NCT * BCOLS], BF16) for i in range(2)]
            tmp1 = [sb1(f"tmp1_{i}", [128, 4 * CH_B], F32) for i in range(2)]
            xk = [sb1(f"xk{i}", [128, NCT * CH_B], BF16) for i in range(2)]
            ekb = [sb1(f"ekb{i}", [128, CH_B], F32) for i in range(2)]

            def ln_tile_d(xa_t, off, sl):
                ln_tile(xa_t, off, sl, lnDump)

            prep_wT(Wk, C, C, wT_k, wnat1)
            wTk3 = wT_k[:].rearrange("p (a r) -> p a r", r=C)
            xnfv = [t[:].rearrange("p (a r) -> p a r", r=BCOLS) for t in xnf]
            xk3 = [t[:].rearrange("p (a r) -> p a r", r=CH_B) for t in xk]
            P.emit("v", lambda e: e.memset(xnf[0][:], 0.0), writes=[xnf[0]])

            def p1_stage_a(ch):
                par = ch % 2
                xnfp, xkp, xnfpv = xnf[par], xk[par], xnfv[par]
                for a in range(4):
                    xbp = xb[a % 2]
                    dma("s", xbp[:],
                        xf[ch * CH_B + a * 128: ch * CH_B + (a + 1) * 128, :])
                    ln_tile_d(xbp, 0, a)
                    tr_set(NCT, xbp,
                           lambda j, xbp=xbp: xbp[:, j * 128:(j + 1) * 128],
                           xnfp,
                           lambda g0, cnt, a=a, xnfpv=xnfpv: xnfpv[
                               :, g0:g0 + cnt, 1 + a * 128:1 + (a + 1) * 128],
                           f32=True)
                if ch > 0:
                    P.emit("v", lambda e, xnfpv=xnfpv, ov=xnfv[1 - par]: e.tensor_copy(
                        xnfpv[:, :, 0:1], ov[:, :, CH_B:CH_B + 1]),
                        reads=[xnf[1 - par]], writes=[xnfp])
                if ch == 0:
                    dma("s", xn_sc[:, :, 0:BCOLS].rearrange("a p r -> p a r"),
                        xnfpv)
                else:
                    dma("s", xn_sc[:, :, 1 + ch * CH_B:1 + (ch + 1) * CH_B]
                        .rearrange("a p r -> p a r"), xnfpv[:, :, 1:BCOLS])
                mix(xnfp, CH_B, BCOLS, muk_t, xkp, tmp1[par], 4)

            def p1_stage_b(ch):
                par = ch % 2
                for ht in range(NCT):
                    bk = psA[ht % 4]
                    proj(wTk3, C, ht, xk3[par], bk, CH_B)
                    eb = ekb[ht % 2]
                    P.emit("a", lambda e, bk=bk, eb=eb: e.activation(
                        eb[:], bk[:, 0:CH_B], AF.Exp, bias=kapb[:, 0:1]),
                        reads=[bk, kapb], writes=[eb])
                    dma("s", ek_sc[ht, :, ch * CH_B:(ch + 1) * CH_B], eb[:])

            for ch in range(NCH + 1):
                if ch < NCH:
                    p1_stage_a(ch)
                if ch >= 1:
                    p1_stage_b(ch - 1)
            P.flush(blk)
        _BAR_()

        with ExitStack() as s2:
            def sb2(name, shape, dt):
                return s2.enter_context(nc.sbuf_tensor(name, shape, dt))
            wT_v = sb2("wT_v", [128, NCT * C], BF16)
            wnat2 = sb2("wnat2", [128, C], BF16)
            xnf2 = [sb2(f"xnf2_{i}", [128, NCT * BCOLS], BF16) for i in range(2)]
            tmp2 = [sb2(f"tmp2_{i}", [128, 4 * CH_B], F32) for i in range(2)]
            xv = [sb2(f"xv{i}", [128, NCT * CH_B], BF16) for i in range(2)]
            ekt = [sb2(f"ekt{i}", [128, CH_B], F32) for i in range(2)]
            vB = [sb2(f"vB{i}", [128, CH_B], F32) for i in range(2)]
            ekvB = [sb2(f"ekvB{i}", [128, CH_B], F32) for i in range(2)]
            Nbuf = [sb2(f"Nbuf{i}", [128, BCOLS], F32) for i in range(2)]
            Dbuf = [sb2(f"Dbuf{i}", [128, BCOLS], F32) for i in range(2)]
            numB = [sb2(f"numB{i}", [128, CH_B], F32) for i in range(2)]
            denB = [sb2(f"denB{i}", [128, CH_B], F32) for i in range(2)]
            yB = [sb2(f"yB{i}", [128, CH_B], F32) for i in range(2)]

            prep_wT(Wv, C, C, wT_v, wnat2)
            wTv3 = wT_v[:].rearrange("p (a r) -> p a r", r=C)
            xnf2v = [t[:].rearrange("p (a r) -> p a r", r=BCOLS) for t in xnf2]
            xv3 = [t[:].rearrange("p (a r) -> p a r", r=CH_B) for t in xv]
            for ch in range(NCH):
                par = ch % 2
                own = ch >= NCH // 2
                dma("s", xnf2v[par],
                    xn_sc[:, :, ch * CH_B: ch * CH_B + BCOLS].rearrange("a p r -> p a r"))
                mix(xnf2[par], CH_B, BCOLS, muv_t, xv[par], tmp2[par], 4)
                if ch == NCH // 2:
                    P.emit("v", lambda e: e.tensor_scalar(
                        stN[:], stN[:], mask_t[:, 0:1], None, ALU.mult),
                        reads=[stN, mask_t], writes=[stN])
                    P.emit("v", lambda e: e.tensor_scalar(
                        stD[:], stD[:], mask_t[:, 0:1], None, ALU.mult),
                        reads=[stD, mask_t], writes=[stD])
                for ht in range(NCT):
                    p2 = ht % 2
                    ek_ = ekt[p2]
                    dma("s", ek_[:], ek_sc[ht, :, ch * CH_B:(ch + 1) * CH_B])
                    decb = dec_t[:, ht:ht + 1].broadcast_to([128, CH_B])
                    Nb, Db = Nbuf[p2], Dbuf[p2]
                    if own:
                        P.emit("a", lambda e, ht=ht, Nb=Nb: e.copy(
                            Nb[:, 0:1], stN[:, ht:ht + 1]), reads=[stN], writes=[Nb])
                        P.emit("a", lambda e, ht=ht, Db=Db: e.copy(
                            Db[:, 0:1], stD[:, ht:ht + 1]), reads=[stD], writes=[Db])
                    bv = psA[ht % 4]
                    proj(wTv3, C, ht, xv3[par], bv, CH_B)
                    vb = vB[p2]
                    P.emit("a", lambda e, bv=bv, vb=vb: e.copy(vb[:], bv[:, 0:CH_B]),
                           reads=[bv], writes=[vb])
                    ekv_ = ekvB[p2]
                    P.emit("v", lambda e, ek_=ek_, vb=vb, ekv_=ekv_: e.tensor_mul(
                        ekv_[:], ek_[:], vb[:]), reads=[ek_, vb], writes=[ekv_])
                    P.emit("v", lambda e, ht=ht, decb=decb, ekv_=ekv_, Nb=Nb:
                           e.tensor_tensor_scan(
                               Nb[:, 1:BCOLS], decb, ekv_[:], stN[:, ht:ht + 1],
                               ALU.mult, ALU.add),
                           reads=[ekv_, dec_t, stN], writes=[Nb])
                    P.emit("a", lambda e, ht=ht, Nb=Nb: e.copy(
                        stN[:, ht:ht + 1], Nb[:, CH_B:CH_B + 1]),
                        reads=[Nb], writes=[stN])
                    P.emit("v", lambda e, ht=ht, decb=decb, ek_=ek_, Db=Db:
                           e.tensor_tensor_scan(
                               Db[:, 1:BCOLS], decb, ek_[:], stD[:, ht:ht + 1],
                               ALU.mult, ALU.add),
                           reads=[ek_, dec_t, stD], writes=[Db])
                    P.emit("a", lambda e, ht=ht, Db=Db: e.copy(
                        stD[:, ht:ht + 1], Db[:, CH_B:CH_B + 1]),
                        reads=[Db], writes=[stD])
                    if own:
                        nm, dn, yb = numB[p2], denB[p2], yB[p2]
                        P.emit("v", lambda e, ht=ht, ekv_=ekv_, Nb=Nb, nm=nm:
                               e.scalar_tensor_tensor(
                                   nm[:], ekv_[:], eu_t[:, ht:ht + 1],
                                   Nb[:, 0:CH_B], ALU.mult, ALU.add),
                               reads=[ekv_, eu_t, Nb], writes=[nm])
                        P.emit("v", lambda e, ht=ht, ek_=ek_, Db=Db, dn=dn:
                               e.scalar_tensor_tensor(
                                   dn[:], ek_[:], eu_t[:, ht:ht + 1],
                                   Db[:, 0:CH_B], ALU.mult, ALU.add),
                               reads=[ek_, eu_t, Db], writes=[dn])
                        P.emit("a", lambda e, dn=dn: e.activation(
                            dn[:], dn[:], AF.Ln), reads=[dn], writes=[dn])
                        P.emit("a", lambda e, dn=dn: e.activation(
                            dn[:], dn[:], AF.Exp, scale=-1.0),
                            reads=[dn], writes=[dn])
                        P.emit("v", lambda e, nm=nm, dn=dn, yb=yb: e.tensor_mul(
                            yb[:], nm[:], dn[:]), reads=[nm, dn], writes=[yb])
                        dma("s", y_sc[ht, :, (ch - NCH // 2) * CH_B:
                                      (ch - NCH // 2 + 1) * CH_B], yb[:])
            P.flush(blk)
        _BAR_()

        with ExitStack() as s3:
            def sb3(name, shape, dt):
                return s3.enter_context(nc.sbuf_tensor(name, shape, dt))
            wT_r = sb3("wT_r", [128, NCT * C], BF16)
            wnat3 = sb3("wnat3", [128, C], BF16)
            xnf3 = [sb3(f"xnf3_{i}", [128, NCT * BCOLS], BF16) for i in range(2)]
            tmp3 = [sb3(f"tmp3_{i}", [128, 4 * CH_B], F32) for i in range(2)]
            xr = [sb3(f"xr{i}", [128, NCT * CH_B], BF16) for i in range(2)]
            srB = [sb3(f"srB{i}", [128, CH_B], F32) for i in range(2)]
            yt = [sb3(f"yt{i}", [128, CH_B], F32) for i in range(2)]
            zt = [sb3(f"zt{i}", [128, CH_B], BF16) for i in range(2)]

            prep_wT(Wr, C, C, wT_r, wnat3)
            wTr3 = wT_r[:].rearrange("p (a r) -> p a r", r=C)
            xnf3v = [t[:].rearrange("p (a r) -> p a r", r=BCOLS) for t in xnf3]
            xr3 = [t[:].rearrange("p (a r) -> p a r", r=CH_B) for t in xr]
            for ch8 in range(NCH_B):
                ch = NCH // 2 + ch8
                par = ch8 % 2
                dma("s", xnf3v[par],
                    xn_sc[:, :, ch * CH_B: ch * CH_B + BCOLS].rearrange("a p r -> p a r"))
                mix(xnf3[par], CH_B, BCOLS, mur_t, xr[par], tmp3[par], 4)
                for ht in range(NCT):
                    p2 = ht % 2
                    ytb = yt[p2]
                    dma("s", ytb[:], y_sc[ht, :, ch8 * CH_B:(ch8 + 1) * CH_B])
                    br = psA[ht % 4]
                    proj(wTr3, C, ht, xr3[par], br, CH_B)
                    sr = srB[p2]
                    P.emit("a", lambda e, br=br, sr=sr: e.activation(
                        sr[:], br[:, 0:CH_B], AF.Sigmoid), reads=[br], writes=[sr])
                    ztb = zt[p2]
                    P.emit("v", lambda e, ztb=ztb, ytb=ytb, sr=sr: e.tensor_mul(
                        ztb[:], sr[:], ytb[:]), reads=[sr, ytb], writes=[ztb])
                    dma("s", z_sc[ht, :, ch8 * CH_B:(ch8 + 1) * CH_B], ztb[:])
            P.flush(blk)
        _BAR_()

        with ExitStack() as s4:
            def sb4(name, shape, dt):
                return s4.enter_context(nc.sbuf_tensor(name, shape, dt))
            wT_o = sb4("wT_o", [128, NCT * C], BF16)
            wnat4 = sb4("wnat4", [128, C], BF16)
            ztl = [sb4(f"ztl{i}", [128, NCT * TB], BF16) for i in range(2)]
            atl = [sb4(f"atl{i}", [128, TB], F32) for i in range(2)]

            prep_wT(Wo, C, C, wT_o, wnat4)
            wTo3 = wT_o[:].rearrange("p (a r) -> p a r", r=C)
            for tb in range(NTB):
                zb = ztl[tb % 2]
                zb3 = zb[:].rearrange("p (a r) -> p a r", r=TB)
                dma("s", zb3,
                    z_sc[:, :, tb * TB:(tb + 1) * TB].rearrange("a p r -> p a r"))
                for dt_i in range(NCT):
                    bo = psA[dt_i % 4]
                    proj(wTo3, C, dt_i, zb3, bo, TB)
                    ab = atl[dt_i % 2]
                    P.emit("v", lambda e, bo=bo, ab=ab: e.tensor_copy(
                        ab[:], bo[:, 0:TB]), reads=[bo], writes=[ab])
                    dma("s", at_sc[dt_i, :, tb * TB:(tb + 1) * TB], ab[:])
            P.flush(blk)
        _BAR_()

        with ExitStack() as s5:
            def sb5(name, shape, dt):
                return s5.enter_context(nc.sbuf_tensor(name, shape, dt))
            xnC = sb5("xnC", [128, NCT * TH], BF16)
            w1cT = [sb5(f"w1cT{i}", [128, NCT * 256], BF16) for i in range(2)]
            w3cT = [sb5(f"w3cT{i}", [128, NCT * 256], BF16) for i in range(2)]
            wnat5 = [sb5(f"wnat5_{i}", [128, 2 * C], BF16) for i in range(2)]
            sgC = [sb5(f"sgC{i}", [128, TB], BF16) for i in range(2)]
            lC = [sb5(f"lC{i}", [128, TB], BF16) for i in range(2)]
            hC = [sb5(f"hC{i}", [128, TB], BF16) for i in range(2)]

            xnCv = xnC[:].rearrange("p (a r) -> p a r", r=TH)
            for q4 in range(4):
                dma("s", xnCv[:, q4 * 4:(q4 + 1) * 4, :],
                    xn_sc[q4 * 4:(q4 + 1) * 4, :, 1 + TH:T + 1].rearrange("a p r -> p a r"))

            w1cT3 = [t[:].rearrange("p (a r) -> p a r", r=256) for t in w1cT]
            w3cT3 = [t[:].rearrange("p (a r) -> p a r", r=256) for t in w3cT]

            def prep_w13(wsrc, wT, wTv, fc, par):
                wn = wnat5[par]
                dma("g", wn[:].rearrange("p (a c) -> p a c", c=C),
                    wsrc[fc * 256: (fc + 1) * 256, :]
                    .rearrange("(a p) c -> p a c", p=128))
                for a in range(2):
                    r0 = a * 128
                    tr_set(NCT, wn,
                           lambda j, a=a, wn=wn: wn[:, a * C + j * 128:
                                                    a * C + (j + 1) * 128],
                           wT,
                           lambda g0, cnt, r0=r0, wTv=wTv: wTv[:, g0:g0 + cnt,
                                                               r0:r0 + 128])

            for fc in range(FF // 256):
                par = fc % 2
                prep_w13(W1, w1cT[par], w1cT3[par], fc, par)
                prep_w13(W3, w3cT[par], w3cT3[par], fc, par)
                for tb in range(NTB):
                    act3 = xnCv[:, :, tb * TB:(tb + 1) * TB]
                    for ft in range(2):
                        bg = psA[2 * (ft % 2)]
                        bl = psA[2 * (ft % 2) + 1]
                        proj(w1cT3[par], 256, ft, act3, bg, TB)
                        proj(w3cT3[par], 256, ft, act3, bl, TB)
                        sg = sgC[ft % 2]
                        lb = lC[ft % 2]
                        hb = hC[ft % 2]
                        P.emit("a", lambda e, bg=bg, sg=sg: e.activation(
                            sg[:], bg[:, 0:TB], AF.Silu), reads=[bg], writes=[sg])
                        P.emit("v", lambda e, bl=bl, lb=lb: e.tensor_copy(
                            lb[:], bl[:, 0:TB]), reads=[bl], writes=[lb])
                        P.emit("v", lambda e, hb=hb, sg=sg, lb=lb: e.tensor_mul(
                            hb[:], sg[:], lb[:]), reads=[sg, lb], writes=[hb])
                        dma("s", hid_sc[fc * 2 + ft, :, tb * TB:(tb + 1) * TB],
                            hb[:])
            P.flush(blk)
        _BAR_()

        with ExitStack() as s6:
            def sb6(name, shape, dt):
                return s6.enter_context(nc.sbuf_tensor(name, shape, dt))
            w2Tq = sb6("w2Tq", [128, NFT * 512], BF16)
            w2nat = sb6("w2nat", [128, FF // 2], BF16)
            bigA = sb6("bigA", [128, 32 * 512], BF16)
            bigB = sb6("bigB", [128, 32 * 512], BF16)
            at4 = sb6("at4", [128, 4 * TB], F32)
            xsl = sb6("xsl", [128, 4 * TB], F32)
            fa = sb6("fa", [128, 4 * TB], F32)
            outt = sb6("outt", [128, 4 * TB], F32)

            w2Tqv = w2Tq[:].rearrange("p (a r) -> p a r", r=512)
            bigT = [bigA, bigB]
            bigv = [t[:].rearrange("p (a r) -> p a r", r=512) for t in bigT]
            for q in range(4):
                for sub in range(4):
                    r0 = q * 512 + sub * 128
                    for half in range(2):
                        dma("g", w2nat[:], W2[r0:r0 + 128,
                                              half * (FF // 2):(half + 1) * (FF // 2)])
                        tr_set(32, w2nat,
                               lambda j: w2nat[:, j * 128:(j + 1) * 128],
                               w2Tq,
                               lambda g0, cnt, half=half, sub=sub: w2Tqv[
                                   :, half * 32 + g0: half * 32 + g0 + cnt,
                                   sub * 128:(sub + 1) * 128])
                for ot in range(NFT):
                    bank = psA[ot % 2]
                    P.emit("t", lambda e, ot=ot, bank=bank: e.matmul(
                        bank[:], h_b[:], w2Tq[:, ot * 512:(ot + 1) * 512],
                        start=True, stop=True),
                        reads=[h_b, w2Tq], writes=[bank])
                    P.emit("a", lambda e, ot=ot, bank=bank: e.copy(
                        w2Tq[:, ot * 512:(ot + 1) * 512], bank[:]),
                        reads=[bank], writes=[w2Tq])
                for lev in range(6):
                    gN = NFT >> (lev + 1)
                    iN = 1 << lev
                    view = w2Tq[:].rearrange("p (g two i d) -> p g two i d",
                                             g=gN, two=2, i=iN)
                    av = view[:, :, 0, :, :]
                    bv = view[:, :, 1, :, :]
                    tv = bigA[:].rearrange("p (g i d) -> p g i d",
                                           g=gN, i=iN)[:, :, :, 0:512]
                    P.emit("v", lambda e, tv=tv, av=av, bv=bv: e.tensor_tensor(
                        tv, av, bv, ALU.subtract), reads=[w2Tq], writes=[bigA])
                    P.emit("v", lambda e, av=av, bv=bv: e.tensor_tensor(
                        av, av, bv, ALU.add), reads=[w2Tq], writes=[w2Tq])
                    P.emit("a", lambda e, bv=bv, tv=tv: e.copy(bv, tv),
                           reads=[bigA], writes=[w2Tq])
                for tb in range(NTB):
                    for hf in range(2):
                        dma("s", bigv[hf],
                            hid_sc[hf * 32:(hf + 1) * 32, :, tb * TB:(tb + 1) * TB]
                            .rearrange("a p r -> p a r"))
                    dma("s", at4[:].rearrange("p (a r) -> p a r", r=TB),
                        at_sc[q * 4:(q + 1) * 4, :, tb * TB:(tb + 1) * TB]
                        .rearrange("a p r -> p a r"))
                    dma("s", xsl[:].rearrange("p (a r) -> p a r", r=TB),
                        x[tb * TB:(tb + 1) * TB, q * 512:(q + 1) * 512]
                        .rearrange("(a p) d -> p a d", p=128))
                    for hf in range(2):
                        for dt_i in range(4):
                            bank = psA[dt_i % 4]

                            def fmm(e, dt_i=dt_i, bank=bank, hf=hf):
                                ins = None
                                src_t = bigT[hf]
                                for oti in range(32):
                                    ot = hf * 32 + oti
                                    ins = e.matmul(
                                        bank[:],
                                        w2Tq[:, ot * 512 + dt_i * 128:
                                             ot * 512 + dt_i * 128 + 128],
                                        src_t[:, oti * 512:(oti + 1) * 512],
                                        start=(ot == 0), stop=(ot == NFT - 1))
                                return ins
                            P.emit("t", fmm, reads=[w2Tq, bigT[hf]], writes=[bank])
                    for dt_i in range(4):
                        bank = psA[dt_i % 4]
                        P.emit("v", lambda e, dt_i=dt_i, bank=bank: e.scalar_tensor_tensor(
                            fa[:, dt_i * TB:(dt_i + 1) * TB], bank[:], RSQ,
                            at4[:, dt_i * TB:(dt_i + 1) * TB], ALU.mult, ALU.add),
                            reads=[bank, at4], writes=[fa])
                    for ts_i in range(4):
                        bank = psT[ts_i % 2]

                        def ftr(e, ts_i=ts_i, bank=bank):
                            ins = None
                            for dt_i in range(4):
                                ins = e.transpose(
                                    bank[:, dt_i * 128:(dt_i + 1) * 128],
                                    fa[:, dt_i * TB + ts_i * 128:
                                       dt_i * TB + ts_i * 128 + 128],
                                    id_f[:])
                            return ins
                        P.emit("t", ftr, reads=[fa, id_f], writes=[bank])
                        P.emit("v", lambda e, ts_i=ts_i, bank=bank: e.tensor_add(
                            outt[:, ts_i * 512:(ts_i + 1) * 512], bank[:],
                            xsl[:, ts_i * 512:(ts_i + 1) * 512]),
                            reads=[bank, xsl], writes=[outt])
                    dma("s", yout[tb * TB:(tb + 1) * TB, q * 512:(q + 1) * 512]
                        .rearrange("(a p) d -> p a d", p=128),
                        outt[:].rearrange("p (a d) -> p a d", d=512))
            for sn in sorted(P.cnt):
                if not sn.startswith("dma_"):
                    continue
                v = P.cnt[sn]
                h = P.sems[sn]
                P.q["s"].append(lambda e, h=h, v=v: e.wait_ge(h, v))
            P.flush(blk)
        _BAR_()

    return nc


_NC_CACHE = {}


def kernel(L0_attn_x, L0_attn_mu_r, L0_attn_mu_k, L0_attn_mu_v, L0_attn_wkv_w,
           L0_attn_wkv_u, L0_attn_Wr, L0_attn_Wk, L0_attn_Wv, L0_attn_Wo,
           L0_ffn_W1, L0_ffn_W2, L0_ffn_W3):
    x_full = np.ascontiguousarray(np.asarray(L0_attn_x, dtype=np.float32))
    w = {k: np.ascontiguousarray(np.asarray(v, dtype=np.float32)) for k, v in dict(
        Wr=L0_attn_Wr, Wk=L0_attn_Wk, Wv=L0_attn_Wv, Wo=L0_attn_Wo,
        W1=L0_ffn_W1, W2=L0_ffn_W2, W3=L0_ffn_W3).items()}
    vecs = {k: np.ascontiguousarray(np.asarray(v, dtype=np.float32)).reshape(1, C)
            for k, v in dict(mu_r=L0_attn_mu_r, mu_k=L0_attn_mu_k,
                             mu_v=L0_attn_mu_v, wkv_w=L0_attn_wkv_w,
                             wkv_u=L0_attn_wkv_u).items()}
    H = hadamard(128)
    I = np.eye(128, dtype=np.float32)

    if "nc" not in _NC_CACHE:
        _NC_CACHE["nc"] = build_nc()
    nc = _NC_CACHE["nc"]

    in_maps = []
    zpref = np.zeros((TH, C), np.float32)
    for c in range(8):
        b, s = c >> 1, c & 1
        t0 = s * TH
        own = np.ascontiguousarray(x_full[b, t0:t0 + TH])
        pref = np.ascontiguousarray(x_full[b, 0:TH]) if s else zpref
        m = dict(x=own,
                 xf=np.ascontiguousarray(np.concatenate([pref, own], axis=0)),
                 maskv=np.full((128, 1), float(s), np.float32),
                 H128in=H, I128in=I, **w, **vecs)
        in_maps.append(m)

    res = run_bass_kernel_spmd(nc, in_maps, core_ids=list(range(8)))
    out = np.empty((B, T, C), dtype=np.float32)
    for c in range(8):
        b, s = c >> 1, c & 1
        out[b, s * TH:(s + 1) * TH] = res.results[c]["y"]
    return out


# version 06 (final)
# speedup vs baseline: 1.3327x; 1.2332x over previous
import math
from contextlib import ExitStack

import numpy as np
import ml_dtypes
import concourse.bass as bass
import concourse.mybir as mybir
from concourse.bass_utils import run_bass_kernel_spmd

F32 = mybir.dt.float32
BF16 = mybir.dt.bfloat16
AF = mybir.ActivationFunctionType
ALU = mybir.AluOpType

B, T, C, FF = 4, 8192, 2048, 8192
TH = T // 2
NCT = C // 128
NFT = FF // 128
KAP = 12.0
RSQ = 1.0 / math.sqrt(FF)
EPS = 1e-5
INV_C = 1.0 / C

CH_A = 256
NCH_A = TH // CH_A
CH_B = 512
NCH_B = TH // CH_B
TB = 512
NTB = TH // TB

ENGS = ("s", "g", "t", "v", "a")


class Prog:

    def __init__(self, nc, sem_alloc):
        self.nc = nc
        self.q = {e: [] for e in ENGS}
        self.sem_alloc = sem_alloc
        self.sems = {}
        self.cnt = {}
        for e in ENGS:
            self.sems[e] = sem_alloc("sem_" + e)
            self.cnt[e] = 0
        self.writer = {}
        self.readers = {}
        self.waited = {e: {} for e in ENGS}

    @staticmethod
    def _key(ap):
        t = getattr(ap, "tensor", ap)
        return getattr(t, "name", str(t))

    def buf_sem(self, name):
        sn = "dma_" + name
        if sn not in self.sems:
            self.sems[sn] = self.sem_alloc(sn)
            self.cnt[sn] = 0
        return sn

    def emit(self, eng, f, reads=(), writes=(), dma=False, dma_buf=None):
        waits = {}

        def need(sem_name, val, src_dma, src_eng):
            if self.waited[eng].get(sem_name, -1) >= val:
                return
            waits[sem_name] = max(waits.get(sem_name, 0), val)

        for r in reads:
            w = self.writer.get(self._key(r))
            if w:
                need(*w)
        for wr in writes:
            k = self._key(wr)
            w = self.writer.get(k)
            if w:
                need(*w)
            for rd in self.readers.get(k, ()):
                need(*rd)
        for sn, v in waits.items():
            self.waited[eng][sn] = max(self.waited[eng].get(sn, -1), v)
        if dma:
            assert dma_buf is not None
            sn = self.buf_sem(self._key(dma_buf))
            d = 16
        else:
            sn = eng
            d = 1
        self.cnt[sn] += d
        val = self.cnt[sn]
        my_sem = self.sems[sn]
        wlist = [(self.sems[s2], v) for s2, v in sorted(waits.items())]

        def g(e, f=f, wlist=wlist, my_sem=my_sem, d=d):
            for (h, v) in wlist:
                e.wait_ge(h, v)
            ins = f(e)
            if ins is not None:
                ins.then_inc(my_sem, d)

        self.q[eng].append(g)
        mark = (sn, val, dma, eng)
        for wr in writes:
            k = self._key(wr)
            self.writer[k] = mark
            self.readers[k] = []
        for r in reads:
            self.readers.setdefault(self._key(r), []).append(mark)
        return mark

    def barrier(self):
        snap = dict(self.cnt)
        for eng in ENGS:
            for sn in sorted(snap):
                v = snap[sn]
                if v <= 0 or self.waited[eng].get(sn, -1) >= v:
                    continue
                h = self.sems[sn]
                self.q[eng].append(lambda e, h=h, v=v: e.wait_ge(h, v))
                self.waited[eng][sn] = v

    def flush(self, blk):
        qs = {e: self.q[e] for e in ENGS}
        self.q = {e: [] for e in ENGS}
        if qs["s"]:
            fs = qs["s"]

            @blk.sync
            def _(e, fs=fs):
                for f in fs:
                    f(e)
        if qs["g"]:
            fg = qs["g"]

            @blk.gpsimd
            def _(e, fg=fg):
                for f in fg:
                    f(e)
        if qs["t"]:
            ft = qs["t"]

            @blk.tensor
            def _(e, ft=ft):
                for f in ft:
                    f(e)
        if qs["v"]:
            fv = qs["v"]

            @blk.vector
            def _(e, fv=fv):
                for f in fv:
                    f(e)
        if qs["a"]:
            fa = qs["a"]

            @blk.scalar
            def _(e, fa=fa):
                for f in fa:
                    f(e)


def hadamard(n):
    h = np.array([[1.0]], dtype=np.float32)
    while h.shape[0] < n:
        h = np.block([[h, h], [h, -h]]).astype(np.float32)
    return h


def build_nc():
    nc = bass.Bass("TRN2", target_bir_lowering=False)

    def din(name, shape):
        return nc.dram_tensor(name, shape, F32, kind="ExternalInput").ap()

    x = din("x", [TH, C])
    xf = din("xf", [T, C])
    maskv = din("maskv", [128, 1])
    Wr = din("Wr", [C, C]); Wk = din("Wk", [C, C]); Wv = din("Wv", [C, C])
    Wo = din("Wo", [C, C])
    W1 = din("W1", [FF, C]); W3 = din("W3", [FF, C]); W2 = din("W2", [C, FF])
    mu_r = din("mu_r", [1, C]); mu_k = din("mu_k", [1, C]); mu_v = din("mu_v", [1, C])
    wkv_w = din("wkv_w", [1, C]); wkv_u = din("wkv_u", [1, C])
    H128in = din("H128in", [128, 128])
    I128in = din("I128in", [128, 128])

    yout = nc.dram_tensor("y", [TH, C], F32, kind="ExternalOutput").ap()

    xn_sc = nc.dram_tensor("xn_sc", [NCT, 128, T + 1], BF16, kind="Internal").ap()
    ek_sc = nc.dram_tensor("ek_sc", [NCT, 128, T], F32, kind="Internal").ap()
    y_sc = nc.dram_tensor("y_sc", [NCT, 128, TH], F32, kind="Internal").ap()
    z_sc = nc.dram_tensor("z_sc", [NCT, 128, TH], BF16, kind="Internal").ap()
    hid_sc = nc.dram_tensor("hid_sc", [NFT, 128, TH], BF16, kind="Internal").ap()
    at_sc = nc.dram_tensor("at_sc", [NCT, 128, TH], F32, kind="Internal").ap()

    with ExitStack() as ctx:
        def sb(name, shape, dt):
            return ctx.enter_context(nc.sbuf_tensor(name, shape, dt))

        def psum(name, shape, dt):
            return ctx.enter_context(nc.psum_tensor(name, shape, dt))

        def sem_alloc(name):
            return ctx.enter_context(nc.semaphore(name))
        P = Prog(nc, sem_alloc)

        def _BAR_():
            P.barrier()

        ones_f = sb("ones_f", [1, 1], F32)
        id_f = sb("id_f", [128, 128], F32)
        id_b = sb("id_b", [128, 128], BF16)
        h_b = sb("h_b", [128, 128], BF16)
        rowp = sb("rowp", [1, C], F32)
        rown = sb("rown", [1, C], F32)
        mur_t = sb("mur_t", [128, NCT], F32)
        muk_t = sb("muk_t", [128, NCT], F32)
        muv_t = sb("muv_t", [128, NCT], F32)
        ww_t = sb("ww_t", [128, NCT], F32)
        wu_t = sb("wu_t", [128, NCT], F32)
        dec_t = sb("dec_t", [128, NCT], F32)
        eu_t = sb("eu_t", [128, NCT], F32)
        mask_t = sb("mask_t", [128, 1], F32)
        stN = sb("stN", [128, NCT], F32)
        stD = sb("stD", [128, NCT], F32)
        lnS = sb("lnS", [128, 8], F32)
        lnQ = sb("lnQ", [128, 8], F32)
        lnM = sb("lnM", [128, 8], F32)
        lnR = sb("lnR", [128, 8], F32)
        lnT = sb("lnT", [128, 8], F32)
        epsb = sb("epsb", [128, 1], F32)
        kapb = sb("kapb", [128, 1], F32)

        psA = [psum(f"psA{i}", [128, 512], F32) for i in range(4)]
        psT = [psum(f"psT{i}", [128, 512], F32) for i in range(2)]
        psB = [psum(f"psB{i}", [128, 1024], BF16) for i in range(2)]

        blk = ctx.enter_context(nc.Block())

        def dma(eng, out, in_, reads=(), writes=()):
            t_out = getattr(out, "tensor", out)
            buf = out if "SB" in type(t_out).__name__ else in_
            P.emit(eng, lambda e, o=out, i=in_: e.dma_start(out=o, in_=i),
                   reads=tuple(reads) + (in_,), writes=tuple(writes) + (out,),
                   dma=True, dma_buf=buf)

        def tr_set(n, src_tensor, src_fn, dst_tensor, dst_fn, f32=False):
            per = 4 if f32 else 8
            banks = psT if f32 else psB
            ident = id_f if f32 else id_b
            ev_eng = "v"
            for g0 in range(0, n, per):
                cnt = min(per, n - g0)
                bank = banks[(g0 // per) % 2]

                def ftr(e, g0=g0, cnt=cnt, bank=bank):
                    ins = None
                    for j in range(cnt):
                        ins = e.transpose(bank[:, j * 128:(j + 1) * 128],
                                          src_fn(g0 + j), ident[:])
                    return ins
                P.emit("t", ftr, reads=[src_tensor, ident], writes=[bank])
                src3 = bank[:].rearrange("p (g r) -> p g r", r=128)[:, 0:cnt, :]
                if ev_eng == "a":
                    P.emit("a", lambda e, d=dst_fn(g0, cnt), s=src3: e.copy(d, s),
                           reads=[bank], writes=[dst_tensor])
                else:
                    P.emit("v", lambda e, d=dst_fn(g0, cnt), s=src3: e.tensor_copy(d, s),
                           reads=[bank], writes=[dst_tensor])

        def prep_wT(wsrc, n_rows, n_cols, wT, wnat):
            nrt = n_rows // 128
            nct_ = n_cols // 128
            wTv = wT[:].rearrange("p (a r) -> p a r", r=n_rows)
            for rt in range(nrt):
                dma("g", wnat[:, 0:n_cols], wsrc[rt * 128:(rt + 1) * 128, :])
                tr_set(nct_, wnat,
                       lambda j, rt=rt: wnat[:, j * 128:(j + 1) * 128],
                       wT,
                       lambda g0, cnt, rt=rt: wTv[:, g0:g0 + cnt,
                                                  rt * 128:(rt + 1) * 128])

        def ln_tile(xa_t, off, sl, lnDump):
            xav = xa_t[:, off:off + C]
            P.emit("a", lambda e: e.activation(lnDump[:], xav, AF.Copy,
                                               accum_out=lnS[:, sl:sl + 1]),
                   reads=[xa_t], writes=[lnDump, lnS])
            P.emit("a", lambda e: e.activation(lnDump[:], xav, AF.Square,
                                               accum_out=lnQ[:, sl:sl + 1]),
                   reads=[xa_t], writes=[lnDump, lnQ])
            P.emit("v", lambda e: e.tensor_scalar(lnM[:, sl:sl + 1],
                                                  lnS[:, sl:sl + 1], INV_C, None,
                                                  ALU.mult),
                   reads=[lnS], writes=[lnM])
            P.emit("v", lambda e: e.tensor_mul(lnT[:, sl:sl + 1], lnM[:, sl:sl + 1],
                                               lnM[:, sl:sl + 1]),
                   reads=[lnM], writes=[lnT])
            P.emit("v", lambda e: e.scalar_tensor_tensor(
                lnQ[:, sl:sl + 1], lnQ[:, sl:sl + 1], INV_C, lnT[:, sl:sl + 1],
                ALU.mult, ALU.subtract), reads=[lnQ, lnT], writes=[lnQ])
            P.emit("a", lambda e: e.activation(lnQ[:, 4 + sl:5 + sl],
                                               lnQ[:, sl:sl + 1], AF.Sqrt,
                                               bias=epsb[:, 0:1]),
                   reads=[lnQ, epsb], writes=[lnQ])
            P.emit("v", lambda e: e.reciprocal(lnR[:, sl:sl + 1],
                                               lnQ[:, 4 + sl:5 + sl]),
                   reads=[lnQ], writes=[lnR])
            P.emit("v", lambda e: e.scalar_tensor_tensor(
                lnT[:, sl:sl + 1], lnM[:, sl:sl + 1], -1.0, lnR[:, sl:sl + 1],
                ALU.mult, ALU.mult), reads=[lnM, lnR], writes=[lnT])
            P.emit("g", lambda e: e.tensor_scalar(
                xav, xav, lnR[:, sl:sl + 1], lnT[:, sl:sl + 1],
                ALU.mult, ALU.add), reads=[xa_t, lnR, lnT], writes=[xa_t])

        def mix(xnf, n, cols, mu_tile, out_bf, tmpf, gper):
            xv = xnf[:].rearrange("p (a r) -> p a r", r=cols)
            ov = out_bf[:].rearrange("p (a r) -> p a r", r=n)
            tv = tmpf[:].rearrange("p (a r) -> p a r", r=n)[:, 0:gper, :]
            for h in range(NCT // gper):
                cs = slice(h * gper, (h + 1) * gper)
                cur = xv[:, cs, 1:n + 1]
                sh = xv[:, cs, 0:n]
                P.emit("g", lambda e, cur=cur, sh=sh, tv=tv: e.tensor_tensor(
                    tv, cur, sh, ALU.subtract),
                    reads=[xnf], writes=[tmpf])
                for j in range(gper):
                    ct = h * gper + j
                    P.emit("g", lambda e, j=j, ct=ct, tv=tv: e.tensor_scalar(
                        tv[:, j, :], tv[:, j, :], mu_tile[:, ct:ct + 1], 0.0,
                        ALU.mult, ALU.add),
                        reads=[tmpf, mu_tile], writes=[tmpf])
                P.emit("g", lambda e, cs=cs, sh=sh, tv=tv: e.tensor_tensor(
                    ov[:, cs, :], tv, sh, ALU.add),
                    reads=[tmpf, xnf], writes=[out_bf])

        def proj(wT3, n_rows, seg0, act3, bank, nw, nk=NCT):
            def f(e):
                ins = None
                for ct in range(nk):
                    ins = e.matmul(bank[:, 0:nw],
                                   wT3[:, ct, seg0 * 128:seg0 * 128 + 128],
                                   act3[:, ct, 0:nw],
                                   start=(ct == 0), stop=(ct == nk - 1))
                return ins
            P.emit("t", f, reads=[wT3, act3], writes=[bank])

        P.emit("v", lambda e: e.memset(ones_f[:], 1.0), writes=[ones_f])
        P.emit("v", lambda e: e.memset(epsb[:], EPS), writes=[epsb])
        P.emit("v", lambda e: e.memset(kapb[:], -KAP), writes=[kapb])
        P.emit("v", lambda e: e.memset(stN[:], 0.0), writes=[stN])
        P.emit("v", lambda e: e.memset(stD[:], 0.0), writes=[stD])
        dma("s", id_f[:], I128in)
        dma("g", id_b[:], I128in)
        dma("g", h_b[:], H128in)
        dma("s", mask_t[:], maskv)
        for (src, dst) in ((mu_r, mur_t), (mu_k, muk_t), (mu_v, muv_t),
                           (wkv_w, ww_t), (wkv_u, wu_t)):
            dma("s", rowp[:], src)

            def fmu(e, rowp=rowp):
                ins = None
                for ct in range(NCT):
                    ins = e.matmul(psT[0][:, ct:ct + 1],
                                   rowp[:, ct * 128:(ct + 1) * 128],
                                   ones_f[:], start=(ct == 0),
                                   stop=(ct == NCT - 1))
                return ins
            P.emit("t", fmu, reads=[rowp, ones_f], writes=[psT[0]])
            P.emit("v", lambda e, dst=dst: e.tensor_copy(dst[:], psT[0][:, 0:NCT]),
                   reads=[psT[0]], writes=[dst])
        P.emit("a", lambda e: e.activation(eu_t[:], ww_t[:], AF.Exp),
               reads=[ww_t], writes=[eu_t])
        P.emit("v", lambda e: e.tensor_copy(dec_t[:], eu_t[:]),
               reads=[eu_t], writes=[dec_t])
        P.emit("a", lambda e: e.activation(dec_t[:], dec_t[:], AF.Exp, scale=-1.0),
               reads=[dec_t], writes=[dec_t])
        P.emit("a", lambda e: e.activation(eu_t[:], wu_t[:], AF.Exp),
               reads=[wu_t], writes=[eu_t])
        P.flush(blk)
        _BAR_()

        NCH = T // CH_B
        BCOLS = CH_B + 1
        with ExitStack() as s1:
            def sb1(name, shape, dt):
                return s1.enter_context(nc.sbuf_tensor(name, shape, dt))
            wT_k = sb1("wT_k", [128, NCT * C], BF16)
            wnat1 = sb1("wnat1", [128, C], BF16)
            lnDump = sb1("lnDump", [128, C], F32)
            xb = [sb1(f"xb{i}", [128, C], F32) for i in range(2)]
            xnf = [sb1(f"xnf{i}", [128, NCT * BCOLS], BF16) for i in range(2)]
            tmp1 = [sb1(f"tmp1_{i}", [128, 4 * CH_B], F32) for i in range(2)]
            xk = [sb1(f"xk{i}", [128, NCT * CH_B], BF16) for i in range(2)]
            ekb = [sb1(f"ekb{i}", [128, CH_B], F32) for i in range(2)]

            def ln_tile_d(xa_t, off, sl):
                ln_tile(xa_t, off, sl, lnDump)

            prep_wT(Wk, C, C, wT_k, wnat1)
            wTk3 = wT_k[:].rearrange("p (a r) -> p a r", r=C)
            xnfv = [t[:].rearrange("p (a r) -> p a r", r=BCOLS) for t in xnf]
            xk3 = [t[:].rearrange("p (a r) -> p a r", r=CH_B) for t in xk]
            P.emit("v", lambda e: e.memset(xnf[0][:], 0.0), writes=[xnf[0]])

            def p1_stage_a(ch):
                par = ch % 2
                xnfp, xkp, xnfpv = xnf[par], xk[par], xnfv[par]
                for a in range(4):
                    xbp = xb[a % 2]
                    dma("s", xbp[:],
                        xf[ch * CH_B + a * 128: ch * CH_B + (a + 1) * 128, :])
                    ln_tile_d(xbp, 0, a)
                    tr_set(NCT, xbp,
                           lambda j, xbp=xbp: xbp[:, j * 128:(j + 1) * 128],
                           xnfp,
                           lambda g0, cnt, a=a, xnfpv=xnfpv: xnfpv[
                               :, g0:g0 + cnt, 1 + a * 128:1 + (a + 1) * 128],
                           f32=True)
                if ch > 0:
                    P.emit("v", lambda e, xnfpv=xnfpv, ov=xnfv[1 - par]: e.tensor_copy(
                        xnfpv[:, :, 0:1], ov[:, :, CH_B:CH_B + 1]),
                        reads=[xnf[1 - par]], writes=[xnfp])
                if ch == 0:
                    dma("s", xn_sc[:, :, 0:BCOLS].rearrange("a p r -> p a r"),
                        xnfpv)
                else:
                    dma("s", xn_sc[:, :, 1 + ch * CH_B:1 + (ch + 1) * CH_B]
                        .rearrange("a p r -> p a r"), xnfpv[:, :, 1:BCOLS])
                mix(xnfp, CH_B, BCOLS, muk_t, xkp, tmp1[par], 4)

            def p1_stage_b(ch):
                par = ch % 2
                for ht in range(NCT):
                    bk = psA[ht % 4]
                    proj(wTk3, C, ht, xk3[par], bk, CH_B)
                    eb = ekb[ht % 2]
                    P.emit("a", lambda e, bk=bk, eb=eb: e.activation(
                        eb[:], bk[:, 0:CH_B], AF.Exp, bias=kapb[:, 0:1]),
                        reads=[bk, kapb], writes=[eb])
                    dma("s", ek_sc[ht, :, ch * CH_B:(ch + 1) * CH_B], eb[:])

            for ch in range(NCH + 1):
                if ch < NCH:
                    p1_stage_a(ch)
                if ch >= 1:
                    p1_stage_b(ch - 1)
            P.flush(blk)
        _BAR_()

        with ExitStack() as s2:
            def sb2(name, shape, dt):
                return s2.enter_context(nc.sbuf_tensor(name, shape, dt))
            wT_v = sb2("wT_v", [128, NCT * C], BF16)
            wnat2 = sb2("wnat2", [128, C], BF16)
            xnf2 = [sb2(f"xnf2_{i}", [128, NCT * BCOLS], BF16) for i in range(2)]
            tmp2 = [sb2(f"tmp2_{i}", [128, 4 * CH_B], F32) for i in range(2)]
            xv = [sb2(f"xv{i}", [128, NCT * CH_B], BF16) for i in range(2)]
            ekt = [sb2(f"ekt{i}", [128, CH_B], F32) for i in range(2)]
            vB = [sb2(f"vB{i}", [128, CH_B], F32) for i in range(2)]
            ekvB = [sb2(f"ekvB{i}", [128, CH_B], F32) for i in range(2)]
            Nbuf = [sb2(f"Nbuf{i}", [128, BCOLS], F32) for i in range(2)]
            Dbuf = [sb2(f"Dbuf{i}", [128, BCOLS], F32) for i in range(2)]
            numB = [sb2(f"numB{i}", [128, CH_B], F32) for i in range(2)]
            denB = [sb2(f"denB{i}", [128, CH_B], F32) for i in range(2)]
            yB = [sb2(f"yB{i}", [128, CH_B], F32) for i in range(2)]

            prep_wT(Wv, C, C, wT_v, wnat2)
            wTv3 = wT_v[:].rearrange("p (a r) -> p a r", r=C)
            xnf2v = [t[:].rearrange("p (a r) -> p a r", r=BCOLS) for t in xnf2]
            xv3 = [t[:].rearrange("p (a r) -> p a r", r=CH_B) for t in xv]
            for ch in range(NCH):
                par = ch % 2
                own = ch >= NCH // 2
                dma("s", xnf2v[par],
                    xn_sc[:, :, ch * CH_B: ch * CH_B + BCOLS].rearrange("a p r -> p a r"))
                mix(xnf2[par], CH_B, BCOLS, muv_t, xv[par], tmp2[par], 4)
                if ch == NCH // 2:
                    P.emit("v", lambda e: e.tensor_scalar(
                        stN[:], stN[:], mask_t[:, 0:1], None, ALU.mult),
                        reads=[stN, mask_t], writes=[stN])
                    P.emit("v", lambda e: e.tensor_scalar(
                        stD[:], stD[:], mask_t[:, 0:1], None, ALU.mult),
                        reads=[stD, mask_t], writes=[stD])
                for ht in range(NCT):
                    p2 = ht % 2
                    ek_ = ekt[p2]
                    dma("s", ek_[:], ek_sc[ht, :, ch * CH_B:(ch + 1) * CH_B])
                    decb = dec_t[:, ht:ht + 1].broadcast_to([128, CH_B])
                    Nb, Db = Nbuf[p2], Dbuf[p2]
                    if own:
                        P.emit("a", lambda e, ht=ht, Nb=Nb: e.copy(
                            Nb[:, 0:1], stN[:, ht:ht + 1]), reads=[stN], writes=[Nb])
                        P.emit("a", lambda e, ht=ht, Db=Db: e.copy(
                            Db[:, 0:1], stD[:, ht:ht + 1]), reads=[stD], writes=[Db])
                    bv = psA[ht % 4]
                    proj(wTv3, C, ht, xv3[par], bv, CH_B)
                    vb = vB[p2]
                    P.emit("a", lambda e, bv=bv, vb=vb: e.copy(vb[:], bv[:, 0:CH_B]),
                           reads=[bv], writes=[vb])
                    ekv_ = ekvB[p2]
                    P.emit("v", lambda e, ek_=ek_, vb=vb, ekv_=ekv_: e.tensor_mul(
                        ekv_[:], ek_[:], vb[:]), reads=[ek_, vb], writes=[ekv_])
                    P.emit("v", lambda e, ht=ht, decb=decb, ekv_=ekv_, Nb=Nb:
                           e.tensor_tensor_scan(
                               Nb[:, 1:BCOLS], decb, ekv_[:], stN[:, ht:ht + 1],
                               ALU.mult, ALU.add),
                           reads=[ekv_, dec_t, stN], writes=[Nb])
                    P.emit("a", lambda e, ht=ht, Nb=Nb: e.copy(
                        stN[:, ht:ht + 1], Nb[:, CH_B:CH_B + 1]),
                        reads=[Nb], writes=[stN])
                    P.emit("v", lambda e, ht=ht, decb=decb, ek_=ek_, Db=Db:
                           e.tensor_tensor_scan(
                               Db[:, 1:BCOLS], decb, ek_[:], stD[:, ht:ht + 1],
                               ALU.mult, ALU.add),
                           reads=[ek_, dec_t, stD], writes=[Db])
                    P.emit("a", lambda e, ht=ht, Db=Db: e.copy(
                        stD[:, ht:ht + 1], Db[:, CH_B:CH_B + 1]),
                        reads=[Db], writes=[stD])
                    if own:
                        nm, dn, yb = numB[p2], denB[p2], yB[p2]
                        P.emit("v", lambda e, ht=ht, ekv_=ekv_, Nb=Nb, nm=nm:
                               e.scalar_tensor_tensor(
                                   nm[:], ekv_[:], eu_t[:, ht:ht + 1],
                                   Nb[:, 0:CH_B], ALU.mult, ALU.add),
                               reads=[ekv_, eu_t, Nb], writes=[nm])
                        P.emit("v", lambda e, ht=ht, ek_=ek_, Db=Db, dn=dn:
                               e.scalar_tensor_tensor(
                                   dn[:], ek_[:], eu_t[:, ht:ht + 1],
                                   Db[:, 0:CH_B], ALU.mult, ALU.add),
                               reads=[ek_, eu_t, Db], writes=[dn])
                        P.emit("a", lambda e, dn=dn: e.activation(
                            dn[:], dn[:], AF.Ln), reads=[dn], writes=[dn])
                        P.emit("a", lambda e, dn=dn: e.activation(
                            dn[:], dn[:], AF.Exp, scale=-1.0),
                            reads=[dn], writes=[dn])
                        P.emit("v", lambda e, nm=nm, dn=dn, yb=yb: e.tensor_mul(
                            yb[:], nm[:], dn[:]), reads=[nm, dn], writes=[yb])
                        dma("s", y_sc[ht, :, (ch - NCH // 2) * CH_B:
                                      (ch - NCH // 2 + 1) * CH_B], yb[:])
            P.flush(blk)
        _BAR_()

        with ExitStack() as s3:
            def sb3(name, shape, dt):
                return s3.enter_context(nc.sbuf_tensor(name, shape, dt))
            wT_r = sb3("wT_r", [128, NCT * C], BF16)
            wnat3 = sb3("wnat3", [128, C], BF16)
            xnf3 = [sb3(f"xnf3_{i}", [128, NCT * BCOLS], BF16) for i in range(2)]
            tmp3 = [sb3(f"tmp3_{i}", [128, 4 * CH_B], F32) for i in range(2)]
            xr = [sb3(f"xr{i}", [128, NCT * CH_B], BF16) for i in range(2)]
            srB = [sb3(f"srB{i}", [128, CH_B], F32) for i in range(2)]
            yt = [sb3(f"yt{i}", [128, CH_B], F32) for i in range(2)]
            zt = [sb3(f"zt{i}", [128, CH_B], BF16) for i in range(2)]

            prep_wT(Wr, C, C, wT_r, wnat3)
            wTr3 = wT_r[:].rearrange("p (a r) -> p a r", r=C)
            xnf3v = [t[:].rearrange("p (a r) -> p a r", r=BCOLS) for t in xnf3]
            xr3 = [t[:].rearrange("p (a r) -> p a r", r=CH_B) for t in xr]
            for ch8 in range(NCH_B):
                ch = NCH // 2 + ch8
                par = ch8 % 2
                dma("s", xnf3v[par],
                    xn_sc[:, :, ch * CH_B: ch * CH_B + BCOLS].rearrange("a p r -> p a r"))
                mix(xnf3[par], CH_B, BCOLS, mur_t, xr[par], tmp3[par], 4)
                for ht in range(NCT):
                    p2 = ht % 2
                    ytb = yt[p2]
                    dma("s", ytb[:], y_sc[ht, :, ch8 * CH_B:(ch8 + 1) * CH_B])
                    br = psA[ht % 4]
                    proj(wTr3, C, ht, xr3[par], br, CH_B)
                    sr = srB[p2]
                    P.emit("a", lambda e, br=br, sr=sr: e.activation(
                        sr[:], br[:, 0:CH_B], AF.Sigmoid), reads=[br], writes=[sr])
                    ztb = zt[p2]
                    P.emit("v", lambda e, ztb=ztb, ytb=ytb, sr=sr: e.tensor_mul(
                        ztb[:], sr[:], ytb[:]), reads=[sr, ytb], writes=[ztb])
                    dma("s", z_sc[ht, :, ch8 * CH_B:(ch8 + 1) * CH_B], ztb[:])
            P.flush(blk)
        _BAR_()

        with ExitStack() as s4:
            def sb4(name, shape, dt):
                return s4.enter_context(nc.sbuf_tensor(name, shape, dt))
            wT_o = sb4("wT_o", [128, NCT * C], BF16)
            wnat4 = sb4("wnat4", [128, C], BF16)
            ztl = [sb4(f"ztl{i}", [128, NCT * TB], BF16) for i in range(2)]
            atl = [sb4(f"atl{i}", [128, TB], F32) for i in range(2)]

            prep_wT(Wo, C, C, wT_o, wnat4)
            wTo3 = wT_o[:].rearrange("p (a r) -> p a r", r=C)
            for tb in range(NTB):
                zb = ztl[tb % 2]
                zb3 = zb[:].rearrange("p (a r) -> p a r", r=TB)
                dma("s", zb3,
                    z_sc[:, :, tb * TB:(tb + 1) * TB].rearrange("a p r -> p a r"))
                for dt_i in range(NCT):
                    bo = psA[dt_i % 4]
                    proj(wTo3, C, dt_i, zb3, bo, TB)
                    ab = atl[dt_i % 2]
                    P.emit("v", lambda e, bo=bo, ab=ab: e.tensor_copy(
                        ab[:], bo[:, 0:TB]), reads=[bo], writes=[ab])
                    dma("s", at_sc[dt_i, :, tb * TB:(tb + 1) * TB], ab[:])
            P.flush(blk)
        _BAR_()

        with ExitStack() as s5:
            def sb5(name, shape, dt):
                return s5.enter_context(nc.sbuf_tensor(name, shape, dt))
            xnC = sb5("xnC", [128, NCT * TH], BF16)
            w1cT = [sb5(f"w1cT{i}", [128, NCT * 256], BF16) for i in range(2)]
            w3cT = [sb5(f"w3cT{i}", [128, NCT * 256], BF16) for i in range(2)]
            wnat5 = [sb5(f"wnat5_{i}", [128, 2 * C], BF16) for i in range(2)]
            sgC = [sb5(f"sgC{i}", [128, TB], BF16) for i in range(2)]
            lC = [sb5(f"lC{i}", [128, TB], BF16) for i in range(2)]
            hC = [sb5(f"hC{i}", [128, TB], BF16) for i in range(2)]

            xnCv = xnC[:].rearrange("p (a r) -> p a r", r=TH)
            for q4 in range(4):
                dma("s", xnCv[:, q4 * 4:(q4 + 1) * 4, :],
                    xn_sc[q4 * 4:(q4 + 1) * 4, :, 1 + TH:T + 1].rearrange("a p r -> p a r"))

            w1cT3 = [t[:].rearrange("p (a r) -> p a r", r=256) for t in w1cT]
            w3cT3 = [t[:].rearrange("p (a r) -> p a r", r=256) for t in w3cT]

            def prep_w13(wsrc, wT, wTv, fc, par):
                wn = wnat5[par]
                dma("g", wn[:].rearrange("p (a c) -> p a c", c=C),
                    wsrc[fc * 256: (fc + 1) * 256, :]
                    .rearrange("(a p) c -> p a c", p=128))
                for a in range(2):
                    r0 = a * 128
                    tr_set(NCT, wn,
                           lambda j, a=a, wn=wn: wn[:, a * C + j * 128:
                                                    a * C + (j + 1) * 128],
                           wT,
                           lambda g0, cnt, r0=r0, wTv=wTv: wTv[:, g0:g0 + cnt,
                                                               r0:r0 + 128])

            for fc in range(FF // 256):
                par = fc % 2
                prep_w13(W1, w1cT[par], w1cT3[par], fc, par)
                prep_w13(W3, w3cT[par], w3cT3[par], fc, par)
                for tb in range(NTB):
                    act3 = xnCv[:, :, tb * TB:(tb + 1) * TB]
                    for ft in range(2):
                        bg = psA[2 * (ft % 2)]
                        bl = psA[2 * (ft % 2) + 1]
                        proj(w1cT3[par], 256, ft, act3, bg, TB)
                        proj(w3cT3[par], 256, ft, act3, bl, TB)
                        sg = sgC[ft % 2]
                        lb = lC[ft % 2]
                        hb = hC[ft % 2]
                        P.emit("a", lambda e, bg=bg, sg=sg: e.activation(
                            sg[:], bg[:, 0:TB], AF.Silu), reads=[bg], writes=[sg])
                        P.emit("v", lambda e, bl=bl, lb=lb: e.tensor_copy(
                            lb[:], bl[:, 0:TB]), reads=[bl], writes=[lb])
                        P.emit("v", lambda e, hb=hb, sg=sg, lb=lb: e.tensor_mul(
                            hb[:], sg[:], lb[:]), reads=[sg, lb], writes=[hb])
                        dma("s", hid_sc[fc * 2 + ft, :, tb * TB:(tb + 1) * TB],
                            hb[:])
            P.flush(blk)
        _BAR_()

        with ExitStack() as s6:
            def sb6(name, shape, dt):
                return s6.enter_context(nc.sbuf_tensor(name, shape, dt))
            w2Tq = sb6("w2Tq", [128, NFT * 512], BF16)
            w2nat = sb6("w2nat", [128, FF // 2], BF16)
            bigA = sb6("bigA", [128, 32 * 512], BF16)
            bigB = sb6("bigB", [128, 32 * 512], BF16)
            at4 = sb6("at4", [128, 4 * TB], F32)
            xsl = sb6("xsl", [128, 4 * TB], F32)
            fa = sb6("fa", [128, 4 * TB], F32)
            outt = sb6("outt", [128, 4 * TB], F32)

            w2Tqv = w2Tq[:].rearrange("p (a r) -> p a r", r=512)
            bigT = [bigA, bigB]
            bigv = [t[:].rearrange("p (a r) -> p a r", r=512) for t in bigT]
            for q in range(4):
                for sub in range(4):
                    r0 = q * 512 + sub * 128
                    for half in range(2):
                        dma("g", w2nat[:], W2[r0:r0 + 128,
                                              half * (FF // 2):(half + 1) * (FF // 2)])
                        tr_set(32, w2nat,
                               lambda j: w2nat[:, j * 128:(j + 1) * 128],
                               w2Tq,
                               lambda g0, cnt, half=half, sub=sub: w2Tqv[
                                   :, half * 32 + g0: half * 32 + g0 + cnt,
                                   sub * 128:(sub + 1) * 128])
                for ot in range(NFT):
                    bank = psA[ot % 2]
                    P.emit("t", lambda e, ot=ot, bank=bank: e.matmul(
                        bank[:], h_b[:], w2Tq[:, ot * 512:(ot + 1) * 512],
                        start=True, stop=True),
                        reads=[h_b, w2Tq], writes=[bank])
                    P.emit("a", lambda e, ot=ot, bank=bank: e.copy(
                        w2Tq[:, ot * 512:(ot + 1) * 512], bank[:]),
                        reads=[bank], writes=[w2Tq])
                for lev in range(6):
                    gN = NFT >> (lev + 1)
                    iN = 1 << lev
                    view = w2Tq[:].rearrange("p (g two i d) -> p g two i d",
                                             g=gN, two=2, i=iN)
                    av = view[:, :, 0, :, :]
                    bv = view[:, :, 1, :, :]
                    tv = bigA[:].rearrange("p (g i d) -> p g i d",
                                           g=gN, i=iN)[:, :, :, 0:512]
                    P.emit("v", lambda e, tv=tv, av=av, bv=bv: e.tensor_tensor(
                        tv, av, bv, ALU.subtract), reads=[w2Tq], writes=[bigA])
                    P.emit("v", lambda e, av=av, bv=bv: e.tensor_tensor(
                        av, av, bv, ALU.add), reads=[w2Tq], writes=[w2Tq])
                    P.emit("a", lambda e, bv=bv, tv=tv: e.copy(bv, tv),
                           reads=[bigA], writes=[w2Tq])
                for tb in range(NTB):
                    for hf in range(2):
                        dma("s", bigv[hf],
                            hid_sc[hf * 32:(hf + 1) * 32, :, tb * TB:(tb + 1) * TB]
                            .rearrange("a p r -> p a r"))
                    dma("s", at4[:].rearrange("p (a r) -> p a r", r=TB),
                        at_sc[q * 4:(q + 1) * 4, :, tb * TB:(tb + 1) * TB]
                        .rearrange("a p r -> p a r"))
                    dma("s", xsl[:].rearrange("p (a r) -> p a r", r=TB),
                        x[tb * TB:(tb + 1) * TB, q * 512:(q + 1) * 512]
                        .rearrange("(a p) d -> p a d", p=128))
                    for hf in range(2):
                        for dt_i in range(4):
                            bank = psA[dt_i % 4]

                            def fmm(e, dt_i=dt_i, bank=bank, hf=hf):
                                ins = None
                                src_t = bigT[hf]
                                for oti in range(32):
                                    ot = hf * 32 + oti
                                    ins = e.matmul(
                                        bank[:],
                                        w2Tq[:, ot * 512 + dt_i * 128:
                                             ot * 512 + dt_i * 128 + 128],
                                        src_t[:, oti * 512:(oti + 1) * 512],
                                        start=(ot == 0), stop=(ot == NFT - 1))
                                return ins
                            P.emit("t", fmm, reads=[w2Tq, bigT[hf]], writes=[bank])
                    for dt_i in range(4):
                        bank = psA[dt_i % 4]
                        P.emit("v", lambda e, dt_i=dt_i, bank=bank: e.scalar_tensor_tensor(
                            fa[:, dt_i * TB:(dt_i + 1) * TB], bank[:], RSQ,
                            at4[:, dt_i * TB:(dt_i + 1) * TB], ALU.mult, ALU.add),
                            reads=[bank, at4], writes=[fa])
                    for ts_i in range(4):
                        bank = psT[ts_i % 2]

                        def ftr(e, ts_i=ts_i, bank=bank):
                            ins = None
                            for dt_i in range(4):
                                ins = e.transpose(
                                    bank[:, dt_i * 128:(dt_i + 1) * 128],
                                    fa[:, dt_i * TB + ts_i * 128:
                                       dt_i * TB + ts_i * 128 + 128],
                                    id_f[:])
                            return ins
                        P.emit("t", ftr, reads=[fa, id_f], writes=[bank])
                        P.emit("v", lambda e, ts_i=ts_i, bank=bank: e.tensor_add(
                            outt[:, ts_i * 512:(ts_i + 1) * 512], bank[:],
                            xsl[:, ts_i * 512:(ts_i + 1) * 512]),
                            reads=[bank, xsl], writes=[outt])
                    dma("s", yout[tb * TB:(tb + 1) * TB, q * 512:(q + 1) * 512]
                        .rearrange("(a p) d -> p a d", p=128),
                        outt[:].rearrange("p (a d) -> p a d", d=512))
            for sn in sorted(P.cnt):
                if not sn.startswith("dma_"):
                    continue
                v = P.cnt[sn]
                h = P.sems[sn]
                P.q["s"].append(lambda e, h=h, v=v: e.wait_ge(h, v))
            P.flush(blk)
        _BAR_()

    return nc


_NC_CACHE = {}


def kernel(L0_attn_x, L0_attn_mu_r, L0_attn_mu_k, L0_attn_mu_v, L0_attn_wkv_w,
           L0_attn_wkv_u, L0_attn_Wr, L0_attn_Wk, L0_attn_Wv, L0_attn_Wo,
           L0_ffn_W1, L0_ffn_W2, L0_ffn_W3):
    x_full = np.ascontiguousarray(np.asarray(L0_attn_x, dtype=np.float32))
    w = {k: np.ascontiguousarray(np.asarray(v, dtype=np.float32)) for k, v in dict(
        Wr=L0_attn_Wr, Wk=L0_attn_Wk, Wv=L0_attn_Wv, Wo=L0_attn_Wo,
        W1=L0_ffn_W1, W2=L0_ffn_W2, W3=L0_ffn_W3).items()}
    vecs = {k: np.ascontiguousarray(np.asarray(v, dtype=np.float32)).reshape(1, C)
            for k, v in dict(mu_r=L0_attn_mu_r, mu_k=L0_attn_mu_k,
                             mu_v=L0_attn_mu_v, wkv_w=L0_attn_wkv_w,
                             wkv_u=L0_attn_wkv_u).items()}
    H = hadamard(128)
    I = np.eye(128, dtype=np.float32)

    if "nc" not in _NC_CACHE:
        _NC_CACHE["nc"] = build_nc()
    nc = _NC_CACHE["nc"]

    in_maps = []
    zpref = np.zeros((TH, C), np.float32)
    for c in range(8):
        b, s = c >> 1, c & 1
        t0 = s * TH
        own = np.ascontiguousarray(x_full[b, t0:t0 + TH])
        pref = np.ascontiguousarray(x_full[b, 0:TH]) if s else zpref
        m = dict(x=own,
                 xf=np.ascontiguousarray(np.concatenate([pref, own], axis=0)),
                 maskv=np.full((128, 1), float(s), np.float32),
                 H128in=H, I128in=I, **w, **vecs)
        in_maps.append(m)

    res = run_bass_kernel_spmd(nc, in_maps, core_ids=list(range(8)))
    out = np.empty((B, T, C), dtype=np.float32)
    for c in range(8):
        b, s = c >> 1, c & 1
        out[b, s * TH:(s + 1) * TH] = res.results[c]["y"]
    return out
